# version 04 (final)
# baseline (speedup 1.0000x reference)
import numpy as np
import ml_dtypes
from contextlib import ExitStack

import concourse.bass as bass
import concourse.mybir as mybir
from concourse.bass_utils import run_bass_kernel_spmd
from concourse.bass_core_rust import IndirectOffsetOnAxis

F32 = mybir.dt.float32
BF16 = mybir.dt.bfloat16
I32 = mybir.dt.int32
U32 = mybir.dt.uint32
AF = mybir.ActivationFunctionType
ALU = mybir.AluOpType
AX = mybir.AxisListType

STRONG_FENCE = False
DEBUG_TAPS = []
L2_NCY = 3
L2_SERIAL = False
SIM_SINGLE_CORE = False
B, S, D = 8, 4096, 1024
DT = D // 128
CH = 512
NCH = S // CH

BF = ml_dtypes.bfloat16


class Sem:
    def __init__(self, ctx, name):
        self.h = ctx.stack.enter_context(ctx.nc.semaphore(name))
        self.n = 0

    def inc(self, ins, k=1):
        ins.then_inc(self.h, k)
        self.n += k
        return ins

    def wait(self, eng):
        if self.n > 0:
            eng.wait_ge(self.h, self.n)

    def wait_at(self, eng, v):
        eng.wait_ge(self.h, v)


class _Scope:
    def __init__(self, ctx):
        self.ctx = ctx
        self.stack = ExitStack()

    def __enter__(self):
        self.stack.__enter__()
        return self

    def __exit__(self, *a):
        return self.stack.__exit__(*a)

    def sbuf(self, shape, dt, name="t"):
        return self.stack.enter_context(
            self.ctx.nc.sbuf_tensor(self.ctx.name(name), shape, dt))

    def psum(self, shape, dt=F32, name="p"):
        return self.stack.enter_context(
            self.ctx.nc.psum_tensor(self.ctx.name(name), shape, dt))


class Ctx:
    def __init__(self, nc, stack, block):
        self.nc = nc
        self.stack = stack
        self.block = block
        self._sems = {}
        self.taps = {}
        self._consts = {}
        self._const_loads = []
        self.uid = 0

    def name(self, p):
        self.uid += 1
        return f"{p}_{self.uid}"

    def sem(self, name):
        if name not in self._sems:
            self._sems[name] = Sem(self, name)
        return self._sems[name]

    def sbuf(self, shape, dt, name="t"):
        return self.stack.enter_context(self.nc.sbuf_tensor(self.name(name), shape, dt))

    def psum(self, shape, dt=F32, name="p"):
        return self.stack.enter_context(self.nc.psum_tensor(self.name(name), shape, dt))

    def scope(self):
        return _Scope(self)

    def dram(self, shape, dt, name="scr"):
        ap = self.nc.dram_tensor(self.name(name), list(shape), dt).ap()
        self.taps[name] = ap
        return ap

    def const(self, key, arr):
        if key in self._consts:
            return self._consts[key][1]
        arr = np.ascontiguousarray(arr)
        dt = {np.dtype(np.float32): F32, np.dtype(BF): BF16,
              np.dtype(np.int32): I32}[arr.dtype]
        dr = self.nc.inline_tensor(arr, self.name("c_" + key)).ap()
        sb = self.stack.enter_context(
            self.nc.sbuf_tensor(self.name("cs_" + key), list(arr.shape), dt,
                                side="right"))
        self._consts[key] = (arr, sb)
        ldc = self.sem("ldc")

        def load_f(sync, sb=sb, dr=dr):
            ldc.inc(sync.dma_start(out=sb[:], in_=dr), 16)
        self.block.sync(load_f)
        return sb

    def const_scoped(self, sc, key, arr):
        arr = np.ascontiguousarray(arr)
        dt = {np.dtype(np.float32): F32, np.dtype(BF): BF16,
              np.dtype(np.int32): I32}[arr.dtype]
        dr = self.nc.inline_tensor(arr, self.name("c_" + key)).ap()
        sb = sc.sbuf(list(arr.shape), dt, "cs_" + key)
        ldc = self.sem("ldc")
        pe, dve, act = self.sem("pe"), self.sem("dve"), self.sem("act")
        pe0, dve0, act0 = pe.n, dve.n, act.n

        def load_f(sync, sb=sb, dr=dr):
            sync.wait_ge(pe.h, pe0)
            sync.wait_ge(dve.h, dve0)
            sync.wait_ge(act.h, act0)
            ldc.inc(sync.dma_start(out=sb[:], in_=dr), 16)
        self.block.sync(load_f)
        return sb

    def ones_col(self):
        return self.const("ones_col", np.ones((128, 1), np.float32))

    def eye16(self):
        return self.const("eye16", np.eye(128, dtype=BF))

    def eyef32(self):
        return self.const("eyef32", np.eye(128, dtype=np.float32))

    def ones_row16(self):
        return self.const("ones_row16", np.ones((1, 128), BF))

    def rt(self, vector, ins):
        dve = self.sem("dve")
        dve.inc(ins)
        dve.wait(vector)
        return ins

    def strong_fence(self):
        sems = list(self._sems.values())
        for sec in [self.block.sync, self.block.tensor, self.block.vector,
                    self.block.scalar, self.block.gpsimd]:
            def f(eng, sems=sems):
                for s in sems:
                    s.wait(eng)
            sec(f)

    def flush_consts(self, q):
        pass



class Walker:

    ENG_OF = {"pe": "tensor", "act": "scalar", "dve": "vector",
              "gql": "gpsimd", "gq": "gpsimd", "ld": "sync"}

    def __init__(self, ctx):
        self.ctx = ctx
        self.sems = {k: ctx.sem(n) for k, n in
                     [("pe", "pe"), ("act", "act"), ("dve", "dve"),
                      ("gql", "q_gpl"), ("gq", "q_gp"), ("ld", "q_sync"),
                      ("ldc", "ldc"), ("cc", "ccol")]}
        self.base = {k: s.n for k, s in self.sems.items()}
        self.marks_final = None

    def run(self, fn, engines=("gpsimd", "tensor", "vector", "scalar", "sync")):
        for en in engines:
            self.cnt = dict(self.base)
            self.marks = {}
            self.last_wait = {}
            self.cur = en
            self.eng = None

            def sec(eng, fn=fn):
                self.eng = eng
                fn(self)
            getattr(self.ctx.block, en)(sec)
            self.marks_final = self.marks
        for k, s in self.sems.items():
            s.n = self.cnt[k]

    def mark(self, key):
        self.marks[key] = dict(self.cnt)

    def val(self, key, c):
        return self.marks[key][c]

    def wait(self, who, counter, val):
        if self.cur != who or val <= 0:
            return
        key = (who, counter)
        if self.last_wait.get(key, -1) >= val:
            return
        self.last_wait[key] = val
        self.eng.wait_ge(self.sems[counter].h, val)

    def wait_now(self, who, counter):
        self.wait(who, counter, self.cnt[counter])

    def _fin(self, ins, counter, k=1):
        ins.then_inc(self.sems[counter].h, k)

    def mm(self, out, lhsT, rhs, start=True, stop=True):
        if self.cur == "tensor":
            self._fin(self.eng.matmul(out, lhsT, rhs, start=start, stop=stop),
                      "pe")
        self.cnt["pe"] += 1

    def tp(self, out, in_, ident):
        if self.cur == "tensor":
            self._fin(self.eng.transpose(out, in_, ident), "pe")
        self.cnt["pe"] += 1

    def act(self, out, in_, func, **kw):
        if self.cur == "scalar":
            self._fin(self.eng.activation(out=out, in_=in_, func=func, **kw),
                      "act")
        self.cnt["act"] += 1

    def dve(self, op, *args, rt=False, **kw):
        if self.cur == "vector":
            ins = getattr(self.eng, op)(*args, **kw)
            self._fin(ins, "dve")
        self.cnt["dve"] += 1
        if rt:
            self.wait("vector", "dve", self.cnt["dve"])

    def dma(self, q, out, in_):
        if self.cur == "gpsimd" and q in ("gql", "gq"):
            self._fin(self.eng.dma_start(out=out, in_=in_), q, 16)
        elif self.cur == "sync" and q == "ld":
            self._fin(self.eng.dma_start(out=out, in_=in_), q, 16)
        self.cnt[q] += 16

    def gp_memset(self, ap, v):
        if self.cur == "gpsimd":
            self.eng.memset(ap, v)

    def gp_collective(self, kind, out_ap, in_ap):
        if self.cur == "gpsimd":
            ins = self.eng.collective_compute(
                kind, ALU.bypass, replica_groups=[list(range(8))],
                ins=[in_ap.opt()], outs=[out_ap.opt()])
            self._fin(ins, "cc", 1)
        self.cnt["cc"] += 1



def emit_ln_rows(ctx, sc, x_dram, m_row, r_row, eps=1e-5):
    block = ctx.block
    ones = ctx.ones_col()
    gq = ctx.sem("q_gp")
    gql = ctx.sem("q_gpl")
    pe = ctx.sem("pe")
    act = ctx.sem("act")
    dve = ctx.sem("dve")
    ldc = ctx.sem("ldc")

    xbuf = [sc.sbuf([128, DT * CH], F32, "ln_x") for _ in range(2)]
    sqb = sc.sbuf([128, DT * CH], F32, "ln_sq")
    mm = sc.sbuf([1, CH], F32, "ln_mm")
    p_sum = sc.psum([1, CH], F32, "ln_ps")
    p_ssq = sc.psum([1, CH], F32, "ln_pq")

    gq0, ld0, pe0, act0, dve0 = gq.n, gql.n, pe.n, act.n, dve.n

    def gp_f(gpsimd):
        gpsimd.wait_ge(gq.h, gq0)
        gpsimd.wait_ge(dve.h, dve0)
        gpsimd.wait_ge(act.h, act0)
        gpsimd.wait_ge(pe.h, pe0)
        for c in range(NCH):
            if c >= 2:
                gpsimd.wait_ge(act.h, act0 + 8 * (c - 1))
                gpsimd.wait_ge(pe.h, pe0 + 16 * (c - 1))
            for d in range(DT):
                gpsimd.dma_start(
                    out=xbuf[c % 2][:, d * CH:(d + 1) * CH],
                    in_=x_dram[128 * d:128 * (d + 1), CH * c:CH * (c + 1)]
                ).then_inc(gql.h, 16)
        gql.n += 16 * NCH * DT
    block.gpsimd(gp_f)

    def act_f(scalar):
        for c in range(NCH):
            scalar.wait_ge(gql.h, ld0 + 16 * DT * (c + 1))
            if c >= 1:
                scalar.wait_ge(pe.h, pe0 + 16 * c)
            for d in range(DT):
                act.inc(scalar.activation(
                    out=sqb[:, d * CH:(d + 1) * CH],
                    in_=xbuf[c % 2][:, d * CH:(d + 1) * CH], func=AF.Square))
    block.scalar(act_f)

    def pe_f(tensor):
        ldc.wait(tensor)
        for c in range(NCH):
            tensor.wait_ge(gql.h, ld0 + 16 * DT * (c + 1))
            if c >= 1:
                tensor.wait_ge(dve.h, dve0 + 5 * c)
            for d in range(DT):
                pe.inc(tensor.matmul(p_sum[:, :], ones[:],
                                     xbuf[c % 2][:, d * CH:(d + 1) * CH],
                                     start=(d == 0), stop=(d == DT - 1)))
            tensor.wait_ge(act.h, act0 + 8 * (c + 1))
            for d in range(DT):
                pe.inc(tensor.matmul(p_ssq[:, :], ones[:],
                                     sqb[:, d * CH:(d + 1) * CH],
                                     start=(d == 0), stop=(d == DT - 1)))
    block.tensor(pe_f)

    def dve_f(vector):
        for c in range(NCH):
            vector.wait_ge(pe.h, pe0 + 16 * (c + 1))
            sl = slice(CH * c, CH * (c + 1))
            ctx.rt(vector, vector.tensor_scalar(m_row[0:1, sl], p_sum[:, :],
                                                1.0 / D, None, ALU.mult, ALU.bypass))
            ctx.rt(vector, vector.tensor_scalar(r_row[0:1, sl], p_ssq[:, :],
                                                1.0 / D, None, ALU.mult, ALU.bypass))
            ctx.rt(vector, vector.tensor_mul(mm[:], m_row[0:1, sl], m_row[0:1, sl]))
            ctx.rt(vector, vector.tensor_sub(r_row[0:1, sl], r_row[0:1, sl], mm[:]))
            dve.inc(vector.tensor_scalar(r_row[0:1, sl], r_row[0:1, sl], eps,
                                         None, ALU.add, ALU.bypass))
    block.vector(dve_f)

    def act2_f(scalar):
        dve.wait(scalar)
        act.inc(scalar.activation(out=r_row[:], in_=r_row[:], func=AF.Sqrt))
    block.scalar(act2_f)

    def dve2_f(vector):
        act.wait(vector)
        dve.inc(vector.reciprocal(r_row[:], r_row[:]))
    block.vector(dve2_f)


def emit_xn(ctx, sc, x_dram, m_row, r_row, xn_dram, out_dt=BF16, gc_cols=None,
            rows16=None):
    block = ctx.block
    onesr16 = ctx.ones_row16()
    gq = ctx.sem("q_gp")
    gql = ctx.sem("q_gpl")
    st = ctx.sem("q_gp")
    pe = ctx.sem("pe")
    dve = ctx.sem("dve")
    DVE_C = 9 if gc_cols is None else 9 + 2 * DT

    xbuf = [sc.sbuf([128, DT * CH], F32, "xn_x") for _ in range(2)]
    ob = [sc.sbuf([128, DT * CH], out_dt, "xn_o") for _ in range(2)]
    if gc_cols is not None:
        gtmp = sc.sbuf([128, 1], F32, "xn_gtmp")
    if rows16 is None:
        mr16 = sc.sbuf([1, S], BF16, "xn_m16")
        rr16 = sc.sbuf([1, S], BF16, "xn_r16")
    else:
        mr16, rr16 = rows16
    mb = sc.sbuf([128, CH], F32, "xn_mb")
    rb = sc.sbuf([128, CH], F32, "xn_rb")
    p_b = sc.psum([128, 2 * CH], F32, "xn_pb")

    ld0, pe0, st0, gq0 = gql.n, pe.n, st.n, gq.n

    def dve_rows_f(vector):
        vector.tensor_copy(mr16[:], m_row[:])
        if gc_cols is not None:
            vector.memset(gc_cols[:], 0.0)
        dve.inc(vector.tensor_copy(rr16[:], r_row[:]))
    block.vector(dve_rows_f)
    dve0 = dve.n

    def gp_f(gpsimd):
        gpsimd.wait_ge(dve.h, dve0)
        gpsimd.wait_ge(gq.h, gq0)
        for c in range(NCH + 1):
            if c < NCH:
                if c >= 2:
                    gpsimd.wait_ge(dve.h, dve0 + DVE_C * (c - 1))
                for d in range(DT):
                    gpsimd.dma_start(
                        out=xbuf[c % 2][:, d * CH:(d + 1) * CH],
                        in_=x_dram[128 * d:128 * (d + 1), CH * c:CH * (c + 1)]
                    ).then_inc(gql.h, 16)
            if xn_dram is not None and c >= 1:
                cs = c - 1
                gpsimd.wait_ge(dve.h, dve0 + DVE_C * (cs + 1))
                for d in range(DT):
                    gpsimd.dma_start(
                        out=xn_dram[128 * d:128 * (d + 1), CH * cs:CH * (cs + 1)],
                        in_=ob[cs % 2][:, d * CH:(d + 1) * CH]).then_inc(st.h, 16)
    block.gpsimd(gp_f)
    gql.n += 16 * NCH * DT
    if xn_dram is not None:
        st.n += 16 * NCH * DT

    def pe_f(tensor):
        dve.wait_at(tensor, dve0)
        ctx.sem("ldc").wait(tensor)
        for c in range(NCH):
            if c >= 1:
                tensor.wait_ge(dve.h, dve0 + DVE_C * (c - 1) + 1)
            sl = slice(CH * c, CH * (c + 1))
            tensor.matmul(p_b[:, 0:CH], onesr16[:], mr16[0:1, sl],
                          start=True, stop=True)
            pe.inc(tensor.matmul(p_b[:, CH:2 * CH], onesr16[:], rr16[0:1, sl],
                                 start=True, stop=True))
    block.tensor(pe_f)

    def dve_f(vector):
        for c in range(NCH):
            vector.wait_ge(pe.h, pe0 + (c + 1))
            vector.tensor_copy(mb[:], p_b[:, 0:CH])
            ctx.rt(vector, vector.tensor_copy(rb[:], p_b[:, CH:2 * CH]))
            vector.wait_ge(gql.h, ld0 + 16 * DT * (c + 1))
            if xn_dram is not None and c >= 2:
                vector.wait_ge(st.h, st0 + 16 * DT * (c - 1))
            for d in range(DT):
                sl = slice(d * CH, (d + 1) * CH)
                vector.tensor_sub(xbuf[c % 2][:, sl], xbuf[c % 2][:, sl], mb[:])
                dve.inc(vector.tensor_mul(ob[c % 2][:, sl], xbuf[c % 2][:, sl], rb[:]))
                if gc_cols is not None:
                    dve.inc(vector.tensor_reduce(gtmp[:], ob[c % 2][:, sl],
                                                 AX.X, ALU.add))
                    dve.wait(vector)
                    dve.inc(vector.tensor_add(gc_cols[:, d:d + 1],
                                              gc_cols[:, d:d + 1], gtmp[:]))
    block.vector(dve_f)




def emit_load_bias_cols(ctx, sc, b_dram, n, name="bias"):
    nt = n // 128
    block = ctx.block
    ld = ctx.sem("q_sync")
    pe = ctx.sem("pe")
    dve = ctx.sem("dve")
    act = ctx.sem("act")
    ldc = ctx.sem("ldc")
    eyef = ctx.eyef32()

    cols = sc.sbuf([128, nt], F32, name + "_cols")
    tsc = _Scope(ctx)
    tsc.__enter__()
    raw = tsc.sbuf([nt, 128], F32, name + "_raw")
    p_t = tsc.psum([128, nt], F32, name + "_pt")

    pe0, dve0, act0 = pe.n, dve.n, act.n

    def sync_f(sync):
        sync.wait_ge(pe.h, pe0)
        sync.wait_ge(dve.h, dve0)
        sync.wait_ge(act.h, act0)
        sync.dma_start(out=raw[:], in_=b_dram.rearrange("(a p) -> a p", p=128)
                       ).then_inc(ld.h, 16)
        ld.n += 16
    block.sync(sync_f)
    ld_tgt = ld.n

    def pe_f(tensor):
        ld.wait_at(tensor, ld_tgt)
        ldc.wait(tensor)
        tensor.wait_ge(dve.h, dve0)
        pe.inc(tensor.transpose(p_t[:, :], raw[:], eyef[0:nt, 0:nt]))
    block.tensor(pe_f)
    pe_tgt = pe.n

    def dve_f(vector):
        pe.wait_at(vector, pe_tgt)
        dve.inc(vector.tensor_copy(cols[:], p_t[:, :]))
    block.vector(dve_f)
    tsc.__exit__(None, None, None)
    return cols


def emit_load_wT(ctx, sc, w_dram, n_out, n_in, name="w", binarize=False):
    no_t, ni_t = n_out // 128, n_in // 128
    block = ctx.block
    gq = ctx.sem("q_gp")
    pe = ctx.sem("pe")
    dve = ctx.sem("dve")
    act = ctx.sem("act")
    ldc = ctx.sem("ldc")
    eye = ctx.eye16()

    sb = sc.sbuf([128, ni_t * no_t * 128], BF16, name + "_T")

    def tile(di, no):
        c = (di * no_t + no) * 128
        return sb[:, c:c + 128]

    tsc = ctx.stack.enter_context(ctx.scope()) if False else _Scope(ctx)
    tsc.__enter__()
    raw16 = [tsc.sbuf([128, n_in], BF16, name + "_r16") for _ in range(2)]
    if binarize:
        rawf = [tsc.sbuf([128, n_in], F32, name + "_rf") for _ in range(2)]
        absb = tsc.sbuf([128, n_in], F32, name + "_ab")
        alpha = tsc.sbuf([128, 2], F32, name + "_al")
    p_t = [tsc.psum([128, 128], BF16, name + "_pt") for _ in range(2)]

    gq0, pe0, act0, dve0 = gq.n, pe.n, act.n, dve.n
    DVE_R = (3 + ni_t) if binarize else ni_t

    def gp_f(gpsimd):
        ctx.flush_consts(gpsimd)
        gpsimd.wait_ge(pe.h, pe0)
        gpsimd.wait_ge(dve.h, dve0)
        gpsimd.wait_ge(act.h, act0)
        for r in range(no_t):
            if r >= 2:
                if binarize:
                    gpsimd.wait_ge(dve.h, dve0 + DVE_R * (r - 2) + 3)
                else:
                    gpsimd.wait_ge(pe.h, pe0 + ni_t * (r - 1))
            dst = rawf[r % 2] if binarize else raw16[r % 2]
            gpsimd.dma_start(out=dst[:], in_=w_dram[128 * r:128 * (r + 1), :]
                             ).then_inc(gq.h, 16)
        gq.n += 16 * no_t
    block.gpsimd(gp_f)

    if binarize:
        def act_f(scalar):
            for r in range(no_t):
                scalar.wait_ge(gq.h, gq0 + 16 * (r + 1))
                if r >= 1:
                    scalar.wait_ge(dve.h, dve0 + DVE_R * (r - 1) + 2)
                act.inc(scalar.activation(out=absb[:], in_=rawf[r % 2][:],
                                          func=AF.Abs))
                act.inc(scalar.activation(out=rawf[r % 2][:], in_=rawf[r % 2][:],
                                          func=AF.Sign))
        block.scalar(act_f)

    def _evac_count(i):
        r, di = divmod(i, ni_t)
        if binarize:
            return dve0 + (3 + ni_t) * r + 3 + di + 1
        return dve0 + ni_t * r + di + 1

    def pe_f(tensor):
        ldc.wait(tensor)
        for r in range(no_t):
            if binarize:
                tensor.wait_ge(dve.h, dve0 + DVE_R * r + 3)
            else:
                tensor.wait_ge(gq.h, gq0 + 16 * (r + 1))
            for di in range(ni_t):
                i = r * ni_t + di
                if i >= 2:
                    tensor.wait_ge(dve.h, _evac_count(i - 2))
                pe.inc(tensor.transpose(p_t[i % 2][:, :],
                                        raw16[r % 2][:, 128 * di:128 * (di + 1)],
                                        eye[:]))
    block.tensor(pe_f)

    def dve_f(vector):
        for r in range(no_t):
            if binarize:
                vector.wait_ge(act.h, act0 + 2 * r + 1)
                dve.inc(vector.tensor_reduce(alpha[:, 0:1], absb[:], AX.X, ALU.add))
                dve.wait(vector)
                ctx.rt(vector, vector.tensor_scalar(alpha[:, 1:2], alpha[:, 0:1],
                                                    1.0 / n_in, None, ALU.mult,
                                                    ALU.bypass))
                vector.wait_ge(act.h, act0 + 2 * r + 2)
                if r >= 2:
                    vector.wait_ge(pe.h, pe0 + ni_t * (r - 1))
                dve.inc(vector.tensor_scalar(raw16[r % 2][:], rawf[r % 2][:],
                                             alpha[:, 1:2], None, ALU.mult,
                                             ALU.bypass))
            for di in range(ni_t):
                i = r * ni_t + di
                vector.wait_ge(pe.h, pe0 + i + 1)
                dve.inc(vector.tensor_copy(tile(di, r), p_t[i % 2][:, :]))
    block.vector(dve_f)

    tsc.__exit__(None, None, None)
    return sb, tile


def emit_load_w_nat(ctx, sc, w_dram, n_in, n_out, name="w"):
    ni_t = n_in // 128
    block = ctx.block
    gq = ctx.sem("q_gp")
    pe = ctx.sem("pe")
    dve = ctx.sem("dve")
    act = ctx.sem("act")
    pe0, dve0, act0 = pe.n, dve.n, act.n
    sb = sc.sbuf([128, ni_t * n_out], BF16, name + "_N")

    def tile(di, no):
        c = di * n_out + no * 128
        w = min(128, n_out - no * 128)
        return sb[:, c:c + w]

    def wide(di, lo, hi):
        return sb[:, di * n_out + lo: di * n_out + hi]

    def gp_f(gpsimd):
        gpsimd.wait_ge(pe.h, pe0)
        gpsimd.wait_ge(dve.h, dve0)
        gpsimd.wait_ge(act.h, act0)
        for r in range(ni_t):
            gpsimd.dma_start(out=sb[:, r * n_out:(r + 1) * n_out],
                             in_=w_dram[128 * r:128 * (r + 1), :]).then_inc(gq.h, 16)
        gq.n += 16 * ni_t
    block.gpsimd(gp_f)
    return sb, tile, wide


def emit_load_wT_small(ctx, sc, w_dram, n_out, n_in, name="ws"):
    ni_t = n_in // 128
    block = ctx.block
    ld = ctx.sem("q_sync")
    pe = ctx.sem("pe")
    dve = ctx.sem("dve")
    ldc = ctx.sem("ldc")
    eyef = ctx.eyef32()

    cols = sc.sbuf([128, ni_t * n_out], BF16, name + "_sT")

    def tile(di):
        return cols[:, di * n_out:(di + 1) * n_out]

    tsc = _Scope(ctx)
    tsc.__enter__()
    raw = tsc.sbuf([n_out, n_in], F32, name + "_raw")
    p_t = [tsc.psum([128, n_out], F32, name + "_pt") for _ in range(2)]

    pe0, dve0, act0 = pe.n, dve.n, ctx.sem("act").n

    def sync_f(sync):
        sync.wait_ge(pe.h, pe0)
        sync.wait_ge(dve.h, dve0)
        sync.wait_ge(ctx.sem("act").h, act0)
        sync.dma_start(out=raw[:], in_=w_dram).then_inc(ld.h, 16)
        ld.n += 16
    block.sync(sync_f)
    ld_tgt = ld.n

    def pe_f(tensor):
        ld.wait_at(tensor, ld_tgt)
        ldc.wait(tensor)
        tensor.wait_ge(dve.h, dve0)
        for di in range(ni_t):
            if di >= 2:
                tensor.wait_ge(dve.h, dve0 + di - 1)
            pe.inc(tensor.transpose(p_t[di % 2][:, :],
                                    raw[:, 128 * di:128 * (di + 1)],
                                    eyef[0:n_out, 0:n_out]))
    block.tensor(pe_f)

    def dve_f(vector):
        for di in range(ni_t):
            vector.wait_ge(pe.h, pe0 + di + 1)
            dve.inc(vector.tensor_copy(tile(di), p_t[di % 2][:, :]))
    block.vector(dve_f)
    tsc.__exit__(None, None, None)
    return cols, tile


def emit_vec_load(ctx, sc, v_dram, n, name="v"):
    ldc = ctx.sem("ldc")
    pe = ctx.sem("pe")
    dve = ctx.sem("dve")
    act = ctx.sem("act")
    pe0, dve0, act0 = pe.n, dve.n, act.n
    sb = sc.sbuf([n, 1], F32, name)

    def sync_f(sync):
        sync.wait_ge(pe.h, pe0)
        sync.wait_ge(dve.h, dve0)
        sync.wait_ge(act.h, act0)
        ldc.inc(sync.dma_start(out=sb[:], in_=v_dram.rearrange("(a b) -> a b", b=1)),
                16)
    ctx.block.sync(sync_f)
    return sb, ldc.n



def emit_l3_ffn(ctx, x_in, x_out, W):
    block = ctx.block
    ld = ctx.sem("q_sync")
    gq = ctx.sem("q_gp")
    pe = ctx.sem("pe")
    act = ctx.sem("act")
    dve = ctx.sem("dve")

    with ctx.scope() as sc:
        xn_dram = ctx.dram([D, S], BF16, "xn")
        with ctx.scope() as sln:
            m_row = sln.sbuf([1, S], F32, "m_row")
            r_row = sln.sbuf([1, S], F32, "r_row")
            with ctx.scope() as sc2:
                emit_ln_rows(ctx, sc2, x_in, m_row, r_row)
            with ctx.scope() as sc2:
                emit_xn(ctx, sc2, x_in, m_row, r_row, xn_dram)
            if DEBUG_TAPS:
                mr_d = ctx.dram([1, S], F32, "m_row_dbg")
                rr_d = ctx.dram([1, S], F32, "r_row_dbg")
                dve = ctx.sem("dve")
                gq = ctx.sem("q_gp")

                def dbg_f(gpsimd):
                    dve.wait(gpsimd)
                    gq.inc(gpsimd.dma_start(out=mr_d, in_=m_row[:]), 16)
                    gq.inc(gpsimd.dma_start(out=rr_d, in_=r_row[:]), 16)
                ctx.block.gpsimd(dbg_f)

        if STRONG_FENCE:
            ctx.strong_fence()
        w1T, w1t = emit_load_wT(ctx, sc, W["w1"], 4096, 1024, "w1", binarize=True)
        if STRONG_FENCE:
            ctx.strong_fence()
        w2T, w2t = emit_load_wT(ctx, sc, W["w2"], 1024, 4096, "w2", binarize=True)
        if STRONG_FENCE:
            ctx.strong_fence()
        b1c = emit_load_bias_cols(ctx, sc, W["b1"], 4096, "b1")
        b2c = emit_load_bias_cols(ctx, sc, W["b2"], 1024, "b2")
        if STRONG_FENCE:
            ctx.strong_fence()
        if "wdump" in DEBUG_TAPS:
            w1d = ctx.nc.dram_tensor("tap_w1T", [128, 8 * 32 * 128], BF16,
                                     kind="ExternalOutput").ap()
            w2d = ctx.nc.dram_tensor("tap_w2T", [128, 32 * 8 * 128], BF16,
                                     kind="ExternalOutput").ap()
            gq_ = ctx.sem("q_gp")
            dve_ = ctx.sem("dve")

            def wdump_f(gpsimd):
                dve_.wait(gpsimd)
                gq_.inc(gpsimd.dma_start(out=w1d, in_=w1T[:]), 16)
                gq_.inc(gpsimd.dma_start(out=w2d, in_=w2T[:]), 16)
            ctx.block.gpsimd(wdump_f)

        FT = 32
        xnb = [sc.sbuf([128, DT * CH], BF16, "xnb") for _ in range(2)]
        xres = [sc.sbuf([128, CH], F32, "xres") for _ in range(2)]
        hsb = sc.sbuf([128, FT * CH], BF16, "hsb")
        yout = [sc.sbuf([128, CH], F32, "yout") for _ in range(2)]
        p_h = [sc.psum([128, CH], F32, "ph") for _ in range(2)]
        p_y = [sc.psum([128, CH], F32, "py") for _ in range(2)]

        gql = ctx.sem("q_gpl")
        ld0, gq0, pe0, act0, dve0 = gql.n, gq.n, pe.n, act.n, dve.n
        PE_C = FT * DT + DT * FT

        def gpl_f(gpsimd):
            gpsimd.wait_ge(gq.h, gq0)
            gpsimd.wait_ge(dve.h, dve0)

            def store(i):
                cs, ds = divmod(i, DT)
                gpsimd.wait_ge(dve.h, dve0 + i + 1)
                gpsimd.dma_start(
                    out=x_out[128 * ds:128 * (ds + 1), CH * cs:CH * (cs + 1)],
                    in_=yout[i % 2][:]).then_inc(gq.h, 16)

            for c in range(NCH):
                if c >= 2:
                    gpsimd.wait_ge(pe.h, pe0 + PE_C * (c - 1))
                for d in range(DT):
                    gpsimd.dma_start(
                        out=xnb[c % 2][:, d * CH:(d + 1) * CH],
                        in_=xn_dram[128 * d:128 * (d + 1), CH * c:CH * (c + 1)]
                    ).then_inc(gql.h, 16)
                for d in range(DT):
                    i = c * DT + d
                    if i >= 2:
                        gpsimd.wait_ge(dve.h, dve0 + i - 1)
                    gpsimd.dma_start(
                        out=xres[i % 2][:],
                        in_=x_in[128 * d:128 * (d + 1), CH * c:CH * (c + 1)]
                    ).then_inc(gql.h, 16)
                    if i >= 2:
                        store(i - 2)
            store(NCH * DT - 2)
            store(NCH * DT - 1)
        block.gpsimd(gpl_f)
        gql.n += 16 * NCH * DT * 2
        gq.n += 16 * NCH * DT

        def pe_f(tensor):
            dve.wait_at(tensor, dve0)
            for c in range(NCH):
                tensor.wait_ge(gql.h, ld0 + 16 * (2 * DT * c + DT))
                for f in range(FT):
                    if c * FT + f >= 2:
                        tensor.wait_ge(act.h, act0 + c * FT + f - 1)
                    for d in range(DT):
                        pe.inc(tensor.matmul(p_h[f % 2][:, :], w1t(d, f),
                                             xnb[c % 2][:, d * CH:(d + 1) * CH],
                                             start=(d == 0), stop=(d == DT - 1)))
                tensor.wait_ge(act.h, act0 + FT * (c + 1))
                for d in range(DT):
                    if c * DT + d >= 2:
                        tensor.wait_ge(dve.h, dve0 + c * DT + d - 1)
                    for f in range(FT):
                        pe.inc(tensor.matmul(p_y[d % 2][:, :], w2t(f, d),
                                             hsb[:, f * CH:(f + 1) * CH],
                                             start=(f == 0), stop=(f == FT - 1)))
        block.tensor(pe_f)

        def act_f(scalar):
            dve.wait_at(scalar, dve0)
            for c in range(NCH):
                for f in range(FT):
                    scalar.wait_ge(pe.h, pe0 + PE_C * c + DT * (f + 1))
                    act.inc(scalar.activation(out=hsb[:, f * CH:(f + 1) * CH],
                                              in_=p_h[f % 2][:, :], func=AF.Gelu,
                                              bias=b1c[:, f:f + 1]))
        block.scalar(act_f)

        def dve_f(vector):
            for c in range(NCH):
                for d in range(DT):
                    i = c * DT + d
                    vector.wait_ge(gql.h, ld0 + 16 * (2 * DT * c + DT + d + 1))
                    if i >= 2:
                        vector.wait_ge(gq.h, gq0 + 16 * (i - 1))
                    vector.wait_ge(pe.h, pe0 + PE_C * c + FT * DT + FT * (d + 1))
                    dve.inc(vector.scalar_tensor_tensor(
                        yout[i % 2][:], p_y[d % 2][:, :],
                        b2c[:, d:d + 1], xres[i % 2][:],
                        ALU.add, ALU.add))
        block.vector(dve_f)




def emit_l2_ffn(ctx, x_in, x_out, W):
    block = ctx.block
    gq = ctx.sem("q_gp")
    gql = ctx.sem("q_gpl")
    pe = ctx.sem("pe")
    act = ctx.sem("act")
    dve = ctx.sem("dve")
    NCY = L2_NCY

    with ctx.scope() as sc:
        gc_cols = sc.sbuf([128, DT], F32, "gc_cols")
        mr16 = sc.sbuf([1, S], BF16, "l2m16")
        rr16 = sc.sbuf([1, S], BF16, "l2r16")
        with ctx.scope() as sln:
            m_row = sln.sbuf([1, S], F32, "m_row")
            r_row = sln.sbuf([1, S], F32, "r_row")
            with ctx.scope() as sc2:
                emit_ln_rows(ctx, sc2, x_in, m_row, r_row)
            with ctx.scope() as sc2:
                emit_xn(ctx, sc2, x_in, m_row, r_row, None,
                        out_dt=F32, gc_cols=gc_cols, rows16=(mr16, rr16))

        cw1, cw1t, _ = emit_load_w_nat(ctx, sc, W["cn_w1"], 1024, 1024, "cw1")
        cw2, cw2t, _ = emit_load_w_nat(ctx, sc, W["cn_w2"], 1024, 512, "cw2")
        cw3, cw3t, _ = emit_load_w_nat(ctx, sc, W["cn_w3"], 512, 256, "cw3")
        cw4, cw4t, _ = emit_load_w_nat(ctx, sc, W["cn_w4"], 256, 1, "cw4")
        gw1, gw1t, _ = emit_load_w_nat(ctx, sc, W["gc_w1"], 1024, 512, "gw1")
        gw2, gw2t, _ = emit_load_w_nat(ctx, sc, W["gc_w2"], 512, 256, "gw2")
        gw3, gw3t, _ = emit_load_w_nat(ctx, sc, W["gc_w3"], 256, 1, "gw3")
        cb1 = emit_load_bias_cols(ctx, sc, W["cn_b1"], 1024, "cb1")
        cb2 = emit_load_bias_cols(ctx, sc, W["cn_b2"], 512, "cb2")
        cb3 = emit_load_bias_cols(ctx, sc, W["cn_b3"], 256, "cb3")
        gb1 = emit_load_bias_cols(ctx, sc, W["gc_b1"], 512, "gb1")
        gb2 = emit_load_bias_cols(ctx, sc, W["gc_b2"], 256, "gb2")
        cb4, _ = emit_vec_load(ctx, sc, W["cn_b4"], 1, "cb4")
        gb3, _ = emit_vec_load(ctx, sc, W["gc_b3"], 1, "gb3")
        arng, _ = emit_vec_load(ctx, sc, W["adaptive_range"], 1, "arng")
        wq_done = gq.n

        gc16 = sc.sbuf([128, DT], BF16, "gc16")
        g1o = sc.sbuf([128, 4], BF16, "g1o")
        g2o = sc.sbuf([128, 2], BF16, "g2o")
        gfA = sc.sbuf([1, 2], F32, "gfAB")
        coefs = sc.sbuf([1, 2], F32, "coefs")
        gsc = _Scope(ctx)
        gsc.__enter__()
        p_s = gsc.psum([1, 1], F32, "ps")
        p_g1 = [gsc.psum([128, 1], F32, "pg1") for _ in range(4)]
        p_g2 = [gsc.psum([128, 1], F32, "pg2") for _ in range(2)]

        def dve_gc(vector):
            ctx.rt(vector, vector.tensor_scalar(gc_cols[:], gc_cols[:], 1.0 / S,
                                                None, ALU.mult, ALU.bypass))
            dve.inc(vector.tensor_copy(gc16[:], gc_cols[:]))
        block.vector(dve_gc)
        gc_ready = dve.n

        def g_pe1(tensor):
            dve.wait_at(tensor, gc_ready)
            tensor.wait_ge(gq.h, wq_done)
            for no in range(4):
                for d in range(DT):
                    pe.inc(tensor.matmul(p_g1[no][:, :], gw1t(d, no),
                                         gc16[:, d:d + 1], start=(d == 0),
                                         stop=(d == DT - 1)))
        block.tensor(g_pe1)
        pe_g1done = pe.n

        def g_act1(scalar):
            pe.wait_at(scalar, pe_g1done)
            for no in range(4):
                act.inc(scalar.activation(out=g1o[:, no:no + 1],
                                          in_=p_g1[no][:, :], func=AF.Gelu,
                                          bias=gb1[:, no:no + 1]))
        block.scalar(g_act1)
        act_g1 = act.n

        def g_pe2(tensor):
            act.wait_at(tensor, act_g1)
            for no in range(2):
                for d in range(4):
                    pe.inc(tensor.matmul(p_g2[no][:, :], gw2t(d, no),
                                         g1o[:, d:d + 1], start=(d == 0),
                                         stop=(d == 3)))
        block.tensor(g_pe2)
        pe_g2done = pe.n

        def g_act2(scalar):
            pe.wait_at(scalar, pe_g2done)
            for no in range(2):
                act.inc(scalar.activation(out=g2o[:, no:no + 1],
                                          in_=p_g2[no][:, :], func=AF.Gelu,
                                          bias=gb2[:, no:no + 1]))
        block.scalar(g_act2)
        act_g2 = act.n

        def g_pe3(tensor):
            act.wait_at(tensor, act_g2)
            for d in range(2):
                pe.inc(tensor.matmul(p_s[:, :], gw3t(d, 0), g2o[:, d:d + 1],
                                     start=(d == 0), stop=(d == 1)))
        block.tensor(g_pe3)
        pe_g3done = pe.n

        def g_act3(scalar):
            pe.wait_at(scalar, pe_g3done)
            ctx.sem("ldc").wait(scalar)
            act.inc(scalar.activation(out=gfA[0:1, 0:1], in_=p_s[:, :],
                                      func=AF.Sigmoid, bias=gb3[0:1, 0:1]))
        block.scalar(g_act3)
        act_gf = act.n

        def dve_coef(vector):
            act.wait_at(vector, act_gf)
            ctx.sem("ldc").wait(vector)
            ctx.rt(vector, vector.tensor_mul(gfA[0:1, 1:2], gfA[0:1, 0:1],
                                             arng[0:1, 0:1]))
            ctx.rt(vector, vector.scalar_tensor_tensor(
                coefs[0:1, 0:1], gfA[0:1, 1:2], 1.4, arng[0:1, 0:1],
                ALU.mult, ALU.subtract))
            ctx.rt(vector, vector.tensor_scalar(coefs[0:1, 0:1], coefs[0:1, 0:1],
                                                0.1, None, ALU.add, ALU.bypass))
            dve.inc(vector.tensor_scalar(coefs[0:1, 1:2], arng[0:1, 0:1], 0.6,
                                         None, ALU.mult, ALU.bypass))
        block.vector(dve_coef)
        gsc.__exit__(None, None, None)
        coef_ready = dve.n

        xch = [sc.sbuf([128, DT * CH], F32, "xch") for _ in range(2)]
        stA = sc.sbuf([128, DT * CH], F32, "stA")
        stB = sc.sbuf([128, DT * CH], F32, "stB")
        xout = [sc.sbuf([128, DT * CH], F32, "xout") for _ in range(2)]
        sst16 = sc.sbuf([128, DT * CH], BF16, "sst16")
        h1 = sc.sbuf([128, DT * CH], BF16, "h1")
        h2 = sc.sbuf([128, 4 * CH], BF16, "h2")
        h3 = sc.sbuf([128, 2 * CH], BF16, "h3")
        trow = sc.sbuf([1, CH], F32, "trow")
        crow16 = sc.sbuf([1, CH], BF16, "crow16")
        cb_t = sc.sbuf([128, CH], F32, "cbt")
        tw = [sc.sbuf([128, CH], F32, "tw") for _ in range(8)]
        mb = sc.sbuf([128, CH], F32, "mb")
        rb = sc.sbuf([128, CH], F32, "rb")
        p_m = [sc.psum([128, CH], F32, "pm") for _ in range(2)]
        p_r = sc.psum([1, CH], F32, "pr")
        p_cb = sc.psum([128, CH], F32, "pcb")
        p_b = sc.psum([128, 2 * CH], F32, "pb")

        ld0, gq0, pe0, act0, dve0 = gql.n, gq.n, pe.n, act.n, dve.n
        PE_CY = 8 * DT + 4 * 8 + 2 * 4 + 2 + 1
        ACT_CY = 8 + 4 + 2 + 1
        DVE_CY = 1 + 1 + 1 + 1 + 8 + 8
        PE_CH = 1 + NCY * PE_CY
        ACT_CH = NCY * ACT_CY
        DVE_CH = 1 + 8 + NCY * DVE_CY + 8

        def pe_b(c):
            return pe0 + PE_CH * c

        def act_b(c):
            return act0 + ACT_CH * c

        def dve_b(c):
            return dve0 + DVE_CH * c

        PART = [4, 5, 6, 7, 0, 1, 2, 3]
        SGN = [1.0, 1.0, -1.0, -1.0, -1.0, -1.0, 1.0, 1.0]

        def gpl_f(gpsimd):
            gpsimd.wait_ge(gq.h, gq0)
            gpsimd.wait_ge(dve.h, dve0)
            for c in range(NCH + 2):
                if c < NCH:
                    if c >= 2:
                        gpsimd.wait_ge(dve.h, dve_b(c - 1))
                    for d in range(DT):
                        gpsimd.dma_start(
                            out=xch[c % 2][:, d * CH:(d + 1) * CH],
                            in_=x_in[128 * d:128 * (d + 1), CH * c:CH * (c + 1)]
                        ).then_inc(gql.h, 16)
                if c >= 2:
                    cs = c - 2
                    gpsimd.wait_ge(dve.h, dve_b(cs + 1))
                    for d in range(DT):
                        gpsimd.dma_start(
                            out=x_out[128 * d:128 * (d + 1), CH * cs:CH * (cs + 1)],
                            in_=xout[cs % 2][:, d * CH:(d + 1) * CH]
                        ).then_inc(gq.h, 16)
        block.gpsimd(gpl_f)
        gql.n += 16 * NCH * DT
        gq.n += 16 * NCH * DT

        def pe_f(tensor):
            dve.wait_at(tensor, coef_ready)
            for c in range(NCH):
                if c >= 1:
                    tensor.wait_ge(dve.h, dve_b(c - 1) + 1)
                tensor.matmul(p_b[:, 0:CH], ctx.ones_row16()[:],
                              mr16[0:1, CH * c:CH * (c + 1)], start=True, stop=True)
                pe.inc(tensor.matmul(p_b[:, CH:2 * CH], ctx.ones_row16()[:],
                                     rr16[0:1, CH * c:CH * (c + 1)],
                                     start=True, stop=True))
                for cy in range(NCY):
                    cyb = pe_b(c) + 1 + PE_CY * cy
                    tensor.wait_ge(dve.h, dve_b(c) + 9 + DVE_CY * cy + 1)
                    for no in range(8):
                        tensor.wait_ge(act.h, act_b(c) + ACT_CY * cy + max(no - 1, 0))
                        for d in range(DT):
                            pe.inc(tensor.matmul(p_m[no % 2][:, :], cw1t(d, no),
                                                 sst16[:, d * CH:(d + 1) * CH],
                                                 start=(d == 0), stop=(d == DT - 1)))
                    for no in range(4):
                        tensor.wait_ge(act.h, act_b(c) + ACT_CY * cy + 8 + max(no - 1, 0))
                        for d in range(8):
                            pe.inc(tensor.matmul(p_m[no % 2][:, :], cw2t(d, no),
                                                 h1[:, d * CH:(d + 1) * CH],
                                                 start=(d == 0), stop=(d == 7)))
                    for no in range(2):
                        tensor.wait_ge(act.h, act_b(c) + ACT_CY * cy + 12 + max(no - 1, 0))
                        for d in range(4):
                            pe.inc(tensor.matmul(p_m[no % 2][:, :], cw3t(d, no),
                                                 h2[:, d * CH:(d + 1) * CH],
                                                 start=(d == 0), stop=(d == 3)))
                    tensor.wait_ge(act.h, act_b(c) + ACT_CY * cy + 14)
                    for d in range(2):
                        pe.inc(tensor.matmul(p_r[:, :], cw4t(d, 0),
                                             h3[:, d * CH:(d + 1) * CH],
                                             start=(d == 0), stop=(d == 1)))
                    tensor.wait_ge(dve.h, dve_b(c) + 9 + DVE_CY * cy + 3)
                    pe.inc(tensor.matmul(p_cb[:, :], ctx.ones_row16()[:],
                                         crow16[0:1, :], start=True, stop=True))
        block.tensor(pe_f)

        def act_f(scalar):
            dve.wait_at(scalar, coef_ready)
            ctx.sem("ldc").wait(scalar)
            for c in range(NCH):
                for cy in range(NCY):
                    cyb = pe_b(c) + 1 + PE_CY * cy
                    for no in range(8):
                        scalar.wait_ge(pe.h, cyb + DT * (no + 1))
                        act.inc(scalar.activation(out=h1[:, no * CH:(no + 1) * CH],
                                                  in_=p_m[no % 2][:, :], func=AF.Gelu,
                                                  bias=cb1[:, no:no + 1]))
                    for no in range(4):
                        scalar.wait_ge(pe.h, cyb + 64 + 8 * (no + 1))
                        act.inc(scalar.activation(out=h2[:, no * CH:(no + 1) * CH],
                                                  in_=p_m[no % 2][:, :], func=AF.Gelu,
                                                  bias=cb2[:, no:no + 1]))
                    for no in range(2):
                        scalar.wait_ge(pe.h, cyb + 96 + 4 * (no + 1))
                        act.inc(scalar.activation(out=h3[:, no * CH:(no + 1) * CH],
                                                  in_=p_m[no % 2][:, :], func=AF.Gelu,
                                                  bias=cb3[:, no:no + 1]))
                    scalar.wait_ge(pe.h, cyb + 104 + 2)
                    act.inc(scalar.activation(out=trow[:], in_=p_r[:, :],
                                              func=AF.Sigmoid, bias=cb4[0:1, 0:1]))
        block.scalar(act_f)

        def dve_f(vector):
            for c in range(NCH):
                vector.wait_ge(pe.h, pe_b(c) + 1)
                vector.tensor_copy(mb[:], p_b[:, 0:CH])
                ctx.rt(vector, vector.tensor_copy(rb[:], p_b[:, CH:2 * CH]))
                vector.wait_ge(gql.h, ld0 + 16 * DT * (c + 1))
                for j in range(DT):
                    sl = slice(j * CH, (j + 1) * CH)
                    vector.tensor_sub(stA[:, sl], xch[c % 2][:, sl], mb[:])
                    dve.inc(vector.tensor_mul(stA[:, sl], stA[:, sl], rb[:]))
                for cy in range(NCY):
                    csrc = stA if cy % 2 == 0 else stB
                    cdst = stB if cy % 2 == 0 else stA
                    if cy >= 1:
                        vector.wait_ge(pe.h, pe_b(c) + 1 + PE_CY * (cy - 1) + 64)
                    elif c >= 1:
                        vector.wait_ge(pe.h, pe_b(c - 1) + 1 + PE_CY * (NCY - 1) + 64)
                    ctx.rt(vector, vector.tensor_copy(sst16[:], csrc[:]))
                    vector.wait_ge(act.h, act_b(c) + ACT_CY * (cy + 1))
                    ctx.rt(vector, vector.tensor_scalar(trow[:], trow[:],
                                                        coefs[0:1, 1:2],
                                                        coefs[0:1, 0:1],
                                                        ALU.mult, ALU.add))
                    ctx.rt(vector, vector.tensor_copy(crow16[:], trow[:]))
                    vector.wait_ge(pe.h, pe_b(c) + 1 + PE_CY * (cy + 1))
                    ctx.rt(vector, vector.tensor_copy(cb_t[:], p_cb[:, :]))
                    for j in range(8):
                        sp_ = slice(PART[j] * CH, (PART[j] + 1) * CH)
                        ctx.rt(vector, vector.tensor_mul(tw[j][:], cb_t[:],
                                                         csrc[:, sp_]))
                    for j in range(8):
                        sj = slice(j * CH, (j + 1) * CH)
                        dve.inc(vector.scalar_tensor_tensor(
                            cdst[:, sj], tw[j][:], SGN[j],
                            csrc[:, sj], ALU.mult, ALU.add))
                fin_st = stA if NCY % 2 == 0 else stB
                if c >= 2:
                    vector.wait_ge(gq.h, gq0 + 16 * DT * (c - 1))
                for j in range(DT):
                    sl = slice(j * CH, (j + 1) * CH)
                    dve.inc(vector.tensor_add(xout[c % 2][:, sl],
                                              xch[c % 2][:, sl], fin_st[:, sl]))
        block.vector(dve_f)



def emit_l1_ffn(ctx, x_in, x_out, W):
    block = ctx.block
    gq = ctx.sem("q_gp")
    gql = ctx.sem("q_gpl")
    pe = ctx.sem("pe")
    act = ctx.sem("act")
    dve = ctx.sem("dve")
    E = 8

    with ctx.scope() as sc:
        xn_dram = ctx.dram([D, S], BF16, "xn")
        gc_cols = sc.sbuf([128, DT], F32, "pool_cols")
        with ctx.scope() as sln:
            m_row = sln.sbuf([1, S], F32, "m_row")
            r_row = sln.sbuf([1, S], F32, "r_row")
            with ctx.scope() as sc2:
                emit_ln_rows(ctx, sc2, x_in, m_row, r_row)
            with ctx.scope() as sc2:
                emit_xn(ctx, sc2, x_in, m_row, r_row, xn_dram, gc_cols=gc_cols)

        gwT, gwTt = emit_load_wT_small(ctx, sc, W["gate_w"], E, 1024, "gw")
        gbv, _ = emit_vec_load(ctx, sc, W["gate_b"], E, "gbv")
        pooled16 = sc.sbuf([128, DT], BF16, "pooled16")
        lcol = sc.sbuf([E, 1], F32, "lcol")
        lr = sc.sbuf([1, E], F32, "lrow")
        rws = sc.sbuf([1, 4 * E], F32, "gate_rws")
        sca = sc.sbuf([1, 8], F32, "gate_sca")
        gates16 = sc.sbuf([1, E], BF16, "gates16")
        gcols = sc.sbuf([128, E], F32, "gcols")
        onesr8 = ctx.const("ones_row8", np.ones((1, E), np.float32))
        gsc = _Scope(ctx)
        gsc.__enter__()
        p_l = gsc.psum([E, 1], F32, "pl")
        p_lr = gsc.psum([1, E], F32, "plr")
        p_gb = gsc.psum([128, E], F32, "pgb")

        def dve_pool(vector):
            ctx.rt(vector, vector.tensor_scalar(gc_cols[:], gc_cols[:], 1.0 / S,
                                                None, ALU.mult, ALU.bypass))
            dve.inc(vector.tensor_copy(pooled16[:], gc_cols[:]))
        block.vector(dve_pool)
        pool_ready = dve.n

        def g_pe1(tensor):
            dve.wait_at(tensor, pool_ready)
            for d in range(DT):
                pe.inc(tensor.matmul(p_l[:, :], gwTt(d), pooled16[:, d:d + 1],
                                     start=(d == 0), stop=(d == DT - 1)))
        block.tensor(g_pe1)
        pe_l = pe.n

        def dve_l(vector):
            pe.wait_at(vector, pe_l)
            ctx.sem("ldc").wait(vector)
            ctx.rt(vector, vector.tensor_add(lcol[:], p_l[:, :], gbv[:]))
        block.vector(dve_l)
        dve_l1 = dve.n

        def pe_tr(tensor):
            dve.wait_at(tensor, dve_l1)
            pe.inc(tensor.transpose(p_lr[:, :], lcol[:], ctx.eyef32()[0:E, 0:E]))
        block.tensor(pe_tr)
        pe_l2 = pe.n

        def dve_gates(vector):
            pe.wait_at(vector, pe_l2)
            ctx.rt(vector, vector.tensor_copy(lr[:], p_lr[:, :]))
            ctx.rt(vector, vector.tensor_reduce(sca[0:1, 0:1], lr[:], AX.X, ALU.max))
            ctx.rt(vector, vector.tensor_scalar(rws[0:1, 0:E], lr[:],
                                                sca[0:1, 0:1], None,
                                                ALU.is_equal, ALU.bypass))
            ctx.rt(vector, vector.tensor_tensor_scan(
                rws[0:1, E:2 * E], onesr8[0:1, :], rws[0:1, 0:E], 0.0,
                ALU.mult, ALU.add))
            ctx.rt(vector, vector.tensor_scalar(rws[0:1, E:2 * E], rws[0:1, E:2 * E],
                                                1.0, None, ALU.is_equal, ALU.bypass))
            ctx.rt(vector, vector.tensor_mul(rws[0:1, 2 * E:3 * E], rws[0:1, 0:E],
                                             rws[0:1, E:2 * E]))
            ctx.rt(vector, vector.scalar_tensor_tensor(
                rws[0:1, 3 * E:4 * E], rws[0:1, 2 * E:3 * E], -1.0e9, lr[:],
                ALU.mult, ALU.add))
            ctx.rt(vector, vector.tensor_reduce(sca[0:1, 2:3],
                                                rws[0:1, 3 * E:4 * E], AX.X, ALU.max))
            ctx.rt(vector, vector.tensor_scalar(rws[0:1, 0:E], lr[:],
                                                sca[0:1, 2:3], None,
                                                ALU.is_ge, ALU.bypass))
            dve.inc(vector.tensor_scalar(sca[0:1, 1:2], sca[0:1, 0:1], -1.0,
                                         None, ALU.mult, ALU.bypass))
        block.vector(dve_gates)
        dve_g1 = dve.n

        def act_e(scalar):
            dve.wait_at(scalar, dve_g1)
            act.inc(scalar.activation(out=lr[:], in_=lr[:], func=AF.Exp,
                                      bias=sca[0:1, 1:2]))
        block.scalar(act_e)
        act_e1 = act.n

        def dve_gates2(vector):
            act.wait_at(vector, act_e1)
            ctx.rt(vector, vector.tensor_mul(lr[:], lr[:], rws[0:1, 0:E]))
            ctx.rt(vector, vector.tensor_reduce(sca[0:1, 3:4], lr[:], AX.X, ALU.add))
            ctx.rt(vector, vector.reciprocal(sca[0:1, 4:5], sca[0:1, 3:4]))
            ctx.rt(vector, vector.tensor_scalar(lr[:], lr[:], sca[0:1, 4:5],
                                                None, ALU.mult, ALU.bypass))
            dve.inc(vector.tensor_copy(gates16[:], lr[:]))
        block.vector(dve_gates2)
        dve_g2 = dve.n

        def pe_gb(tensor):
            dve.wait_at(tensor, dve_g2)
            pe.inc(tensor.matmul(p_gb[:, :], ctx.ones_row16()[:], gates16[:],
                                 start=True, stop=True))
        block.tensor(pe_gb)
        pe_gb1 = pe.n

        def dve_gb(vector):
            pe.wait_at(vector, pe_gb1)
            ctx.rt(vector, vector.tensor_copy(gcols[:], p_gb[:, :]))
        block.vector(dve_gb)
        gsc.__exit__(None, None, None)
        gates_ready = dve.n

        ew_flat = W["expert_w"].rearrange("e o d -> (e o) d")
        eb_flat = W["expert_b"].rearrange("e o -> (e o)")
        ewt = []
        ebc = []
        for e in range(E):
            _, t = emit_load_wT(ctx, sc, ew_flat[e * 1024:(e + 1) * 1024, :],
                                1024, 1024, f"ew{e}")
            ewt.append(t)
            ebc.append(emit_load_bias_cols(ctx, sc, eb_flat[e * 1024:(e + 1) * 1024],
                                           1024, f"eb{e}"))

        xnb = [sc.sbuf([128, DT * CH], BF16, "xnb") for _ in range(2)]
        xres = [sc.sbuf([128, CH], F32, "xres") for _ in range(2)]
        ge = [sc.sbuf([128, CH], BF16, "ge") for _ in range(2)]
        accb = sc.sbuf([128, DT * CH], F32, "accb")
        xout = [sc.sbuf([128, CH], F32, "xout") for _ in range(2)]
        p_h = [sc.psum([128, CH], F32, "ph") for _ in range(2)]

        ld0, gq0, pe0, act0, dve0 = gql.n, gq.n, pe.n, act.n, dve.n
        PE_C = E * DT * DT
        ACT_C = E * DT
        DVE_C = E * DT + DT

        def gpl_f(gpsimd):
            gpsimd.wait_ge(gq.h, gq0)
            gpsimd.wait_ge(dve.h, dve0)

            def store(i):
                cs, ds = divmod(i, DT)
                gpsimd.wait_ge(dve.h, dve0 + DVE_C * cs + E * DT + ds + 1)
                gpsimd.dma_start(
                    out=x_out[128 * ds:128 * (ds + 1), CH * cs:CH * (cs + 1)],
                    in_=xout[i % 2][:]).then_inc(gq.h, 16)

            for c in range(NCH):
                if c >= 2:
                    gpsimd.wait_ge(pe.h, pe0 + PE_C * (c - 1))
                for d in range(DT):
                    gpsimd.dma_start(
                        out=xnb[c % 2][:, d * CH:(d + 1) * CH],
                        in_=xn_dram[128 * d:128 * (d + 1), CH * c:CH * (c + 1)]
                    ).then_inc(gql.h, 16)
                for d in range(DT):
                    i = c * DT + d
                    if i >= 2:
                        gpsimd.wait_ge(dve.h, dve0 + DVE_C * c + E * DT + d - 1) \
                            if False else gpsimd.wait_ge(
                                dve.h, _l1f_add_done(dve0, DVE_C, i - 2))
                    gpsimd.dma_start(
                        out=xres[i % 2][:],
                        in_=x_in[128 * d:128 * (d + 1), CH * c:CH * (c + 1)]
                    ).then_inc(gql.h, 16)
                    if i >= 2:
                        store(i - 2)
            store(NCH * DT - 2)
            store(NCH * DT - 1)
        block.gpsimd(gpl_f)
        gql.n += 16 * NCH * DT * 2
        gq.n += 16 * NCH * DT

        def pe_f(tensor):
            dve.wait_at(tensor, dve0)
            for c in range(NCH):
                tensor.wait_ge(gql.h, ld0 + 16 * (2 * DT * c + DT))
                for e in range(E):
                    for no in range(DT):
                        k = c * E * DT + e * DT + no
                        if k >= 2:
                            tensor.wait_ge(act.h, act0 + k - 1)
                        for d in range(DT):
                            pe.inc(tensor.matmul(p_h[no % 2][:, :], ewt[e](d, no),
                                                 xnb[c % 2][:, d * CH:(d + 1) * CH],
                                                 start=(d == 0), stop=(d == DT - 1)))
        block.tensor(pe_f)

        def act_f(scalar):
            dve.wait_at(scalar, dve0)
            for c in range(NCH):
                for e in range(E):
                    for no in range(DT):
                        k = c * E * DT + e * DT + no
                        scalar.wait_ge(pe.h, pe0 + DT * (k + 1))
                        if k >= 2:
                            scalar.wait_ge(dve.h, dve0 + _l1f_acc_done(
                                dve0, DVE_C, k - 2) - dve0)
                        act.inc(scalar.activation(out=ge[k % 2][:],
                                                  in_=p_h[no % 2][:, :],
                                                  func=AF.Gelu,
                                                  bias=ebc[e][:, no:no + 1]))
        block.scalar(act_f)

        def dve_f(vector):
            for c in range(NCH):
                for e in range(E):
                    for no in range(DT):
                        k = c * E * DT + e * DT + no
                        vector.wait_ge(act.h, act0 + k + 1)
                        sl = slice(no * CH, (no + 1) * CH)
                        if e == 0:
                            dve.inc(vector.tensor_scalar(
                                accb[:, sl], ge[k % 2][:], gcols[:, 0:1], None,
                                ALU.mult, ALU.bypass))
                        else:
                            dve.inc(vector.scalar_tensor_tensor(
                                accb[:, sl], ge[k % 2][:], gcols[:, e:e + 1],
                                accb[:, sl], ALU.mult, ALU.add))
                for d in range(DT):
                    i = c * DT + d
                    vector.wait_ge(gql.h, ld0 + 16 * (2 * DT * c + DT + d + 1))
                    if i >= 2:
                        vector.wait_ge(gq.h, gq0 + 16 * (i - 1))
                    dve.inc(vector.tensor_add(xout[i % 2][:],
                                              xres[i % 2][:],
                                              accb[:, d * CH:(d + 1) * CH]))
        block.vector(dve_f)


def _l1f_add_done(dve0, DVE_C, i):
    c, d = divmod(i, DT)
    return dve0 + DVE_C * c + 8 * DT + d + 1


def _l1f_acc_done(dve0, DVE_C, k):
    c, r = divmod(k, 8 * DT)
    return dve0 + DVE_C * c + r + 1



def emit_l1_attn(ctx, x_in, x_out, W):
    block = ctx.block
    gq = ctx.sem("q_gp")
    gql = ctx.sem("q_gpl")
    pe = ctx.sem("pe")
    act = ctx.sem("act")
    dve = ctx.sem("dve")
    HT = 16

    with ctx.scope() as sc:
        xn_dram = ctx.dram([D, S], BF16, "xn")
        with ctx.scope() as sln:
            m_row = sln.sbuf([1, S], F32, "m_row")
            r_row = sln.sbuf([1, S], F32, "r_row")
            with ctx.scope() as sc2:
                emit_ln_rows(ctx, sc2, x_in, m_row, r_row)
            with ctx.scope() as sc2:
                emit_xn(ctx, sc2, x_in, m_row, r_row, xn_dram)

        whg, whgt, _ = emit_load_w_nat(ctx, sc, W["W_hg"], 1024, 4096, "whg")
        wout, woutt, _ = emit_load_w_nat(ctx, sc, W["W_out"], 2048, 1024, "wout")

        xnb = [sc.sbuf([128, DT * CH], BF16, "xnb") for _ in range(2)]
        xres = [sc.sbuf([128, CH], F32, "xres") for _ in range(2)]
        zsb = sc.sbuf([128, HT * CH], F32, "zsb")
        vals = sc.sbuf([128, HT * CH], F32, "vals")
        h16 = sc.sbuf([128, HT * CH], BF16, "h16")
        hstate = sc.sbuf([128, HT], F32, "hstate")
        hsc = sc.sbuf([128, CH], F32, "hsc")
        xout = [sc.sbuf([128, CH], F32, "xout") for _ in range(2)]
        p_g = [sc.psum([128, CH], F32, "pg") for _ in range(2)]
        p_y = [sc.psum([128, CH], F32, "py") for _ in range(2)]

        ld0, gq0, pe0, act0, dve0 = gql.n, gq.n, pe.n, act.n, dve.n
        PE_C = 32 * DT + DT * HT
        ACT_C = HT
        DVE_HID = 2
        DVE_SC = 3
        DVE_C = HT * (DVE_HID + DVE_SC) + DT

        def dve_init(vector):
            ctx.rt(vector, vector.memset(hstate[:], 0.0))
        block.vector(dve_init)
        dve0 = dve.n

        def gpl_f(gpsimd):
            gpsimd.wait_ge(gq.h, gq0)
            gpsimd.wait_ge(dve.h, dve0)

            def store(i):
                cs, ds = divmod(i, DT)
                gpsimd.wait_ge(dve.h, dve0 + DVE_C * cs + HT * (DVE_HID + DVE_SC) + ds + 1)
                gpsimd.dma_start(
                    out=x_out[128 * ds:128 * (ds + 1), CH * cs:CH * (cs + 1)],
                    in_=xout[i % 2][:]).then_inc(gq.h, 16)

            for c in range(NCH):
                if c >= 2:
                    gpsimd.wait_ge(pe.h, pe0 + PE_C * (c - 1))
                for d in range(DT):
                    gpsimd.dma_start(
                        out=xnb[c % 2][:, d * CH:(d + 1) * CH],
                        in_=xn_dram[128 * d:128 * (d + 1), CH * c:CH * (c + 1)]
                    ).then_inc(gql.h, 16)
                for d in range(DT):
                    i = c * DT + d
                    if i >= 2:
                        gpsimd.wait_ge(dve.h, dve0 + DVE_C * ((i - 2) // DT)
                                       + HT * (DVE_HID + DVE_SC) + (i - 2) % DT + 1)
                    gpsimd.dma_start(
                        out=xres[i % 2][:],
                        in_=x_in[128 * d:128 * (d + 1), CH * c:CH * (c + 1)]
                    ).then_inc(gql.h, 16)
                    if i >= 2:
                        store(i - 2)
            store(NCH * DT - 2)
            store(NCH * DT - 1)
        block.gpsimd(gpl_f)
        gql.n += 16 * NCH * DT * 2
        gq.n += 16 * NCH * DT

        def pe_f(tensor):
            dve.wait_at(tensor, dve0)
            for c in range(NCH):
                tensor.wait_ge(gql.h, ld0 + 16 * (2 * DT * c + DT))
                for j in range(32):
                    no = 16 + j if j < 16 else j - 16
                    if c * 32 + j >= 2:
                        jj = c * 32 + j - 2
                        cc, j2 = divmod(jj, 32)
                        if j2 < 16:
                            tensor.wait_ge(act.h, act0 + ACT_C * cc + j2 + 1)
                        else:
                            tensor.wait_ge(dve.h, dve0 + DVE_C * cc
                                           + DVE_HID * (j2 - 16) + 1)
                    for d in range(DT):
                        pe.inc(tensor.matmul(p_g[j % 2][:, :], whgt(d, no),
                                             xnb[c % 2][:, d * CH:(d + 1) * CH],
                                             start=(d == 0), stop=(d == DT - 1)))
                tensor.wait_ge(dve.h, dve0 + DVE_C * c
                               + HT * (DVE_HID + DVE_SC))
                for no in range(DT):
                    k = c * DT + no
                    if k >= 2:
                        tensor.wait_ge(dve.h, _l1a_add_done(dve0, DVE_C, k - 2))
                    for f in range(HT):
                        pe.inc(tensor.matmul(p_y[no % 2][:, :], woutt(f, no),
                                             h16[:, f * CH:(f + 1) * CH],
                                             start=(f == 0), stop=(f == HT - 1)))
        block.tensor(pe_f)

        def act_f(scalar):
            for c in range(NCH):
                for j in range(16):
                    scalar.wait_ge(pe.h, pe0 + PE_C * c + DT * (j + 1))
                    if c >= 1:
                        scalar.wait_ge(dve.h, dve0 + DVE_C * (c - 1)
                                       + HT * DVE_HID + 3 * j + 1)
                    act.inc(scalar.activation(out=zsb[:, j * CH:(j + 1) * CH],
                                              in_=p_g[j % 2][:, :],
                                              func=AF.Sigmoid))
        block.scalar(act_f)

        def dve_f(vector):
            for c in range(NCH):
                for f in range(HT):
                    j = 16 + f
                    vector.wait_ge(pe.h, pe0 + PE_C * c + DT * (j + 1))
                    vector.wait_ge(act.h, act0 + ACT_C * c + min(f + 1, HT))
                    sl = slice(f * CH, (f + 1) * CH)
                    vector.tensor_scalar(vals[:, sl], p_g[j % 2][:, :], 0.0, 0.5,
                                         ALU.max, ALU.add)
                    dve.inc(vector.tensor_mul(vals[:, sl], vals[:, sl],
                                              zsb[:, sl]))
                    dve.inc(vector.tensor_scalar(zsb[:, sl], zsb[:, sl], -1.0,
                                                 1.0, ALU.mult, ALU.add))
                if c >= 1:
                    vector.wait_ge(pe.h, pe0 + PE_C * c)
                for f in range(HT):
                    sl = slice(f * CH, (f + 1) * CH)
                    ctx.rt(vector, vector.tensor_tensor_scan(
                        hsc[:], zsb[:, sl], vals[:, sl],
                        hstate[:, f:f + 1], ALU.mult, ALU.add))
                    ctx.rt(vector, vector.tensor_copy(hstate[:, f:f + 1],
                                                      hsc[:, CH - 1:CH]))
                    dve.inc(vector.tensor_copy(h16[:, sl], hsc[:]))
                for d in range(DT):
                    i = c * DT + d
                    vector.wait_ge(gql.h, ld0 + 16 * (2 * DT * c + DT + d + 1))
                    vector.wait_ge(pe.h, pe0 + PE_C * c + 32 * DT + HT * (d + 1))
                    if i >= 2:
                        vector.wait_ge(gq.h, gq0 + 16 * (i - 1))
                    dve.inc(vector.tensor_add(xout[i % 2][:], p_y[d % 2][:, :],
                                              xres[i % 2][:]))
        block.vector(dve_f)


def _l1a_add_done(dve0, DVE_C, i):
    c, d = divmod(i, DT)
    return dve0 + DVE_C * c + 16 * 5 + d + 1



def _rope_tables():
    th = 10000.0 ** (-np.arange(0, 64, 2, dtype=np.float64) / 64)
    t = np.arange(S, dtype=np.float64)
    ang = np.outer(th, t)
    cos = np.cos(ang)
    sin = np.sin(ang)
    cosd = np.zeros((128, S), np.float64)
    sind = np.zeros((128, S), np.float64)
    for r in range(128):
        i = (r % 64) // 2
        cosd[r] = cos[i]
        sind[r] = -sin[i] if (r % 2 == 0) else sin[i]
    return cosd.astype(np.float32), sind.astype(np.float32)


def _mask_A(p):
    m = np.zeros((128, 512), np.float32)
    m[:, 128 * (p + 1):] = -1.0e5
    m[:, 128 * p:128 * (p + 1)] = np.where(
        np.arange(128)[None, :] > np.arange(128)[:, None], -1.0e5, 0.0)
    return m.astype(BF)


def _mask_B(p):
    r = np.arange(128)[:, None]
    cl = np.arange(512)[None, :]
    m = np.where(r <= cl - 128 * p, 0.0, -1.0e5)
    return m.astype(BF)


def emit_softmax_attn(ctx, x_in, x_out, W, causal, out_proj, fixed_scale):
    block = ctx.block
    dve = ctx.sem("dve")
    pe = ctx.sem("pe")
    NH = 16
    NT = S // 128
    NQM = S // 512
    cosd, sind = _rope_tables()

    with ctx.scope() as sc:
        xn_dram = ctx.dram([D, S], BF16, "xn")
        q_dram = ctx.dram([D, S], BF16, "qh")
        k_dram = ctx.dram([D, S], BF16, "kh")
        vt_dram = ctx.dram([S, D], BF16, "vt")
        with ctx.scope() as sln:
            m_row = sln.sbuf([1, S], F32, "m_row")
            r_row = sln.sbuf([1, S], F32, "r_row")
            with ctx.scope() as sc2:
                emit_ln_rows(ctx, sc2, x_in, m_row, r_row)
            with ctx.scope() as sc2:
                emit_xn(ctx, sc2, x_in, m_row, r_row, xn_dram)

        wT = {}
        for wn in (["Wq", "Wk", "Wv"] + (["Wo"] if out_proj else [])):
            _, wT[wn] = emit_load_wT(ctx, sc, W[wn], 1024, 1024, wn)
        bc = {}
        for bn in (["bq", "bk", "bv"] + (["bo"] if out_proj else [])):
            bc[bn] = emit_load_bias_cols(ctx, sc, W[bn], 1024, bn)
        scol = sc.sbuf([128, 1], F32, "scol")
        if fixed_scale is None:
            svec, _ = emit_vec_load(ctx, sc, W["scale"], 1, "svec")
            s16 = sc.sbuf([1, 1], BF16, "s16")
            with ctx.scope() as stmp:
                p_sc = stmp.psum([128, 1], F32, "psc")

                def dve_s0(vector):
                    ctx.sem("ldc").wait(vector)
                    dve.inc(vector.tensor_copy(s16[:], svec[0:1, 0:1]))
                block.vector(dve_s0)
                s_c = dve.n

                def pe_s(tensor):
                    dve.wait_at(tensor, s_c)
                    pe.inc(tensor.matmul(p_sc[:, :], ctx.ones_row16()[:], s16[:],
                                         start=True, stop=True))
                block.tensor(pe_s)
                s_p = pe.n

                def dve_s(vector):
                    pe.wait_at(vector, s_p)
                    ctx.rt(vector, vector.tensor_copy(scol[:], p_sc[:, :]))
                block.vector(dve_s)
        else:
            def dve_s(vector):
                ctx.rt(vector, vector.memset(scol[:], fixed_scale))
            block.vector(dve_s)

        cosc = ctx.const_scoped(sc, "rope_cos", cosd)
        sinc = ctx.const_scoped(sc, "rope_sin", sind)
        eye = ctx.eye16()
        mA = [ctx.const_scoped(sc, f"maskA{p}", _mask_A(p))
              for p in range(4)] if causal else None
        mB = [ctx.const_scoped(sc, f"maskB{p}", _mask_B(p))
              for p in range(4)] if causal else None

        Osb = [sc.sbuf([128, S], BF16, "osb") for _ in range(DT)]
        SH = [(i + 1 if i % 2 == 0 else i - 1) for i in range(32)]

        with ctx.scope() as sp:
            xnb = [sp.sbuf([128, DT * CH], BF16, "xnb") for _ in range(2)]
            ptmp = sp.sbuf([128, CH], F32, "ptmp")
            swp = sp.sbuf([128, CH], F32, "swp")
            ra = sp.sbuf([128, CH], F32, "ra")
            rb_ = sp.sbuf([128, CH], F32, "rb_")
            qk16 = [sp.sbuf([128, CH], BF16, "qk16") for _ in range(4)]
            v16 = [sp.sbuf([128, CH], BF16, "v16") for _ in range(2)]
            vtt = [sp.sbuf([128, 128], BF16, "vtt") for _ in range(4)]
            p_p = [sp.psum([128, CH], F32, "pp") for _ in range(2)]
            p_t = [sp.psum([128, 128], BF16, "ptr") for _ in range(2)]

            def walkP(w):
                w.wait("gpsimd", "gq", w.base["gq"])
                w.wait("gpsimd", "dve", w.base["dve"])
                w.wait("tensor", "dve", w.base["dve"])
                w.wait("tensor", "ldc", w.base["ldc"])
                w.wait("vector", "ldc", w.base["ldc"])
                for c in range(NCH):
                    sl = slice(CH * c, CH * (c + 1))
                    if c >= 2:
                        w.wait("gpsimd", "pe", w.val(("pc", c - 2), "pe"))
                    for d in range(DT):
                        w.dma("gql", xnb[c % 2][:, d * CH:(d + 1) * CH],
                              xn_dram[128 * d:128 * (d + 1), sl])
                    w.mark(("pl", c))
                    for n in range(DT):
                        for ti, wn in enumerate(["Wq", "Wk"]):
                            g = 2 * (c * DT + n) + ti
                            w.wait("tensor", "gql", w.val(("pl", c), "gql"))
                            if ("pp", g - 2) in w.marks:
                                w.wait("tensor", "dve", w.val(("pp", g - 2), "dve"))
                            for d2 in range(DT):
                                w.mm(p_p[g % 2][:, :], wT[wn](d2, n),
                                     xnb[c % 2][:, d2 * CH:(d2 + 1) * CH],
                                     start=(d2 == 0), stop=(d2 == DT - 1))
                            w.mark(("pmm", g))
                            w.wait("vector", "pe", w.val(("pmm", g), "pe"))
                            bi = (2 * n + ti) % 4
                            if ("qkst", g - 4) in w.marks:
                                w.wait("vector", "gq", w.val(("qkst", g - 4), "gq"))
                            if ti == 0:
                                w.dve("tensor_scalar", ptmp[:], p_p[g % 2][:, :],
                                      bc["bq"][:, n:n + 1], scol[:, 0:1],
                                      ALU.add, ALU.mult)
                            else:
                                w.dve("tensor_scalar", ptmp[:], p_p[g % 2][:, :],
                                      bc["bk"][:, n:n + 1], None,
                                      ALU.add, ALU.bypass)
                            w.mark(("pp", g))
                            w.dve("stream_shuffle", swp[:], ptmp[:], SH, rt=True)
                            w.dve("tensor_mul", ra[:], ptmp[:], cosc[:, sl])
                            w.dve("tensor_mul", rb_[:], swp[:], sinc[:, sl], rt=True)
                            w.dve("tensor_add", qk16[bi][:], ra[:], rb_[:])
                            w.mark(("rope", g))
                            w.wait("gpsimd", "dve", w.val(("rope", g), "dve"))
                            dst = q_dram if ti == 0 else k_dram
                            w.dma("gq", dst[128 * n:128 * (n + 1), sl], qk16[bi][:])
                            w.mark(("qkst", g))
                    w.wait("tensor", "dve",
                           w.val(("pp", 2 * (c * DT + DT - 1) + 1), "dve"))
                    for n in range(DT):
                        g = c * DT + n
                        w.wait("tensor", "gql", w.val(("pl", c), "gql"))
                        if ("vev", g - 2) in w.marks:
                            w.wait("tensor", "dve", w.val(("vev", g - 2), "dve"))
                        for d2 in range(DT):
                            w.mm(p_p[g % 2][:, :], wT["Wv"](d2, n),
                                 xnb[c % 2][:, d2 * CH:(d2 + 1) * CH],
                                 start=(d2 == 0), stop=(d2 == DT - 1))
                        w.mark(("vmm", g))
                        w.wait("vector", "pe", w.val(("vmm", g), "pe"))
                        if ("vtp", (g - 1, 3)) in w.marks:
                            w.wait("vector", "pe", w.val(("vtp", (g - 1, 3)), "pe"))
                        w.dve("tensor_scalar", v16[g % 2][:], p_p[g % 2][:, :],
                              bc["bv"][:, n:n + 1], None, ALU.add, ALU.bypass)
                        w.mark(("vev", g))
                        for j in range(4):
                            m = 4 * g + j
                            w.wait("tensor", "dve", w.val(("vev", g), "dve"))
                            if ("vte", m - 2) in w.marks:
                                w.wait("tensor", "dve", w.val(("vte", m - 2), "dve"))
                            w.tp(p_t[m % 2][:, :],
                                 v16[g % 2][:, 128 * j:128 * (j + 1)], eye[:])
                            w.mark(("vtp", (g, j)))
                            w.wait("vector", "pe", w.val(("vtp", (g, j)), "pe"))
                            if ("vtst", m - 4) in w.marks:
                                w.wait("vector", "gq", w.val(("vtst", m - 4), "gq"))
                            w.dve("tensor_copy", vtt[m % 4][:], p_t[m % 2][:, :])
                            w.mark(("vte", m))
                            w.wait("gpsimd", "dve", w.val(("vte", m), "dve"))
                            w.dma("gq", vt_dram[CH * c + 128 * j:CH * c + 128 * (j + 1),
                                               128 * n:128 * (n + 1)], vtt[m % 4][:])
                            w.mark(("vtst", m))
                    w.mark(("pc", c))

            wk = Walker(ctx)
            wk.run(walkP)
            pP_done = {k: wk.cnt[k] for k in ("gq", "dve", "pe")}

        with ctx.scope() as sa:
            kaug = sa.sbuf([65, S], BF16, "kaug")
            qhead = sa.sbuf([64, S], BF16, "qhead")
            vaug = sa.sbuf([128, NT * 65], BF16, "vaug")
            qaug = sa.sbuf([65, 512], BF16, "qaug")
            mrow = sa.sbuf([1, S], BF16, "mrow")
            mcols = sa.sbuf([128, 2], F32, "mcols")
            m16 = sa.sbuf([128, 1], BF16, "m16c")
            pT16 = [sa.sbuf([128, 1024], BF16, "pT16") for _ in range(2)]
            rl = sa.sbuf([1, 512], F32, "rl")
            rbs = sa.sbuf([64, 512], F32, "rbs")
            p_big = sa.psum([128, 2048], F32, "pbig")
            p_O = sa.psum([65, 512], F32, "pO")
            p_rb = sa.psum([64, 512], F32, "prb")
            p_mr = sa.psum([1, 128], BF16, "pmr")
            onesrf = ctx.const("ones_rowf32", np.ones((1, 128), np.float32))

            def walkA(w):
                w.wait("gpsimd", "gq", w.base["gq"])
                w.wait("gpsimd", "dve", w.base["dve"])
                w.wait("gpsimd", "pe", w.base["pe"])
                w.wait("tensor", "dve", w.base["dve"])
                w.wait("tensor", "ldc", w.base["ldc"])
                w.gp_memset(kaug[64:65, :], -1.0)
                w.gp_memset(vaug[:], 1.0)
                for h in range(NH):
                    if ("hdone", h - 1) in w.marks:
                        w.wait("gpsimd", "pe", w.val(("hdone", h - 1), "pe"))
                        w.wait("gpsimd", "dve", w.val(("hdone", h - 1), "dve"))
                    w.dma("gql", kaug[0:64, :], k_dram[64 * h:64 * (h + 1), :])
                    w.dma("gql", qhead[:, :], q_dram[64 * h:64 * (h + 1), :])
                    for kt in range(NT):
                        w.dma("gql", vaug[:, 65 * kt:65 * kt + 64],
                              vt_dram[128 * kt:128 * (kt + 1), 64 * h:64 * (h + 1)])
                    w.mark(("hl", h))
                    w.wait("tensor", "gql", w.val(("hl", h), "gql"))
                    if ("hexp", h - 1) in w.marks:
                        w.wait("tensor", "act", w.val(("hexp", h - 1), "act"))
                    for qt in range(NT):
                        n512 = (qt + 4) // 4 if causal else 8
                        nf = (n512 + 3) // 4
                        for f in range(nf):
                            wdt = min(4, n512 - 4 * f)
                            if ("par", (qt, f - 1)) in w.marks:
                                w.wait("tensor", "dve",
                                       w.val(("par", (qt, f - 1)), "dve"))
                            elif ("par", (qt - 1, 0)) in w.marks:
                                lastf = (((qt - 1) + 4) // 4 + 3) // 4 - 1 if causal else 1
                                w.wait("tensor", "dve",
                                       w.val(("par", (qt - 1, lastf)), "dve"))
                            for u in range(wdt):
                                kc = 4 * f + u
                                last = (kc == n512 - 1)
                                w.mm(p_big[:, 512 * u:512 * (u + 1)],
                                     qhead[:, 128 * qt:128 * (qt + 1)],
                                     kaug[0:64, 512 * kc:512 * (kc + 1)],
                                     start=True, stop=(not (causal and last)))
                                if causal and last:
                                    w.mm(p_big[:, 512 * u:512 * (u + 1)],
                                         eye[:], mA[qt % 4][:],
                                         start=False, stop=True)
                            w.mark(("paf", (qt, f)))
                            w.wait("vector", "pe", w.val(("paf", (qt, f)), "pe"))
                            w.dve("reduce_max", mcols[:, f:f + 1],
                                  p_big[:, 0:512 * wdt], AX.X, rt=True)
                            w.mark(("par", (qt, f)))
                        if nf > 1:
                            w.dve("reduce_max", m16[:], mcols[:, 0:nf],
                                  AX.X, rt=True)
                        else:
                            w.dve("tensor_copy", m16[:], mcols[:, 0:1], rt=True)
                        w.mark(("pam", qt))
                        w.wait("tensor", "dve", w.val(("pam", qt), "dve"))
                        if ("pamr", qt - 1) in w.marks:
                            w.wait("tensor", "dve", w.val(("pamr", qt - 1), "dve"))
                        w.tp(p_mr[:, :], m16[:], eye[:])
                        w.mark(("pat", qt))
                        w.wait("vector", "pe", w.val(("pat", qt), "pe"))
                        w.dve("tensor_copy", mrow[0:1, 128 * qt:128 * (qt + 1)],
                              p_mr[:, :])
                        w.mark(("pamr", qt))
                    for qm in range(NQM):
                        nkt = 4 * qm + 4 if causal else NT
                        qsl = slice(512 * qm, 512 * (qm + 1))
                        w.wait("vector", "dve", w.val(("pamr", NT - 1), "dve"))
                        if ("bO", (h, qm - 1)) in w.marks:
                            w.wait("vector", "pe", w.val(("bO", (h, qm - 1)), "pe"))
                        w.dve("tensor_copy", qaug[0:64, :], qhead[:, qsl])
                        w.dve("tensor_copy", qaug[64:65, :], mrow[0:1, qsl], rt=True)
                        w.mark(("qa", (h, qm)))
                        for kp in range(nkt // 2):
                            for u in range(2):
                                kt = 2 * kp + u
                                half = slice(512 * (2 * (kp % 2) + u),
                                             512 * (2 * (kp % 2) + u + 1))
                                w.wait("tensor", "dve", w.val(("qa", (h, qm)), "dve"))
                                if ("bexp", (h, qm, kp - 2)) in w.marks:
                                    w.wait("tensor", "act",
                                           w.val(("bexp", (h, qm, kp - 2)), "act"))
                                w.mm(p_big[:, half], kaug[:, 128 * kt:128 * (kt + 1)],
                                     qaug[:, :], start=True,
                                     stop=(not (causal and kt >= 4 * qm)))
                                if causal and kt >= 4 * qm:
                                    w.mm(p_big[:, half], eye[:],
                                         mB[kt - 4 * qm][:], start=False, stop=True)
                            w.mark(("bmm", (h, qm, kp)))
                            w.wait("scalar", "pe", w.val(("bmm", (h, qm, kp)), "pe"))
                            if ("bpv", (h, qm, kp - 2)) in w.marks:
                                w.wait("scalar", "pe",
                                       w.val(("bpv", (h, qm, kp - 2)), "pe"))
                            ph = slice(1024 * (kp % 2), 1024 * (kp % 2 + 1))
                            w.act(pT16[kp % 2][:], p_big[:, ph], AF.Exp)
                            w.mark(("bexp", (h, qm, kp)))
                            w.wait("tensor", "act", w.val(("bexp", (h, qm, kp)), "act"))
                            if kp == 0 and ("bfin", (h, qm - 1)) in w.marks:
                                w.wait("tensor", "dve",
                                       w.val(("bfin", (h, qm - 1)), "dve"))
                            for u in range(2):
                                kt = 2 * kp + u
                                w.mm(p_O[:, :], vaug[:, 65 * kt:65 * (kt + 1)],
                                     pT16[kp % 2][:, 512 * u:512 * (u + 1)],
                                     start=(kt == 0), stop=(kt == nkt - 1))
                            w.mark(("bpv", (h, qm, kp)))
                        w.mark(("bO", (h, qm)))
                        w.wait("vector", "pe", w.val(("bO", (h, qm)), "pe"))
                        w.dve("reciprocal", rl[:], p_O[64:65, :], rt=True)
                        w.mark(("brec", (h, qm)))
                        w.wait("tensor", "dve", w.val(("brec", (h, qm)), "dve"))
                        w.mm(p_rb[:, :], onesrf[0:1, 0:64], rl[:],
                             start=True, stop=True)
                        w.mark(("brb", (h, qm)))
                        w.wait("vector", "pe", w.val(("brb", (h, qm)), "pe"))
                        w.dve("tensor_copy", rbs[:], p_rb[:, :], rt=True)
                        hr = (h % 2) * 64
                        w.dve("tensor_mul",
                              Osb[h // 2][hr:hr + 64, qsl],
                              p_O[0:64, :], rbs[:])
                        w.mark(("bfin", (h, qm)))
                    w.mark(("hdone", h))
                    w.mark(("hexp", h))

            wk = Walker(ctx)
            wk.run(walkA)

        with ctx.scope() as so:
            xres = [so.sbuf([128, CH], F32, "xres") for _ in range(2)]
            xout = [so.sbuf([128, CH], F32, "xout") for _ in range(2)]
            otmp = so.sbuf([128, CH], F32, "otmp")
            p_o = [so.psum([128, CH], F32, "po") for _ in range(2)]

            def walkO(w):
                w.wait("gpsimd", "gq", w.base["gq"])
                w.wait("gpsimd", "dve", w.base["dve"])
                w.wait("tensor", "dve", w.base["dve"])
                w.wait("tensor", "ldc", w.base["ldc"])
                for c in range(NCH):
                    sl = slice(CH * c, CH * (c + 1))
                    for n in range(DT):
                        i = c * DT + n
                        if ("oadd", i - 2) in w.marks:
                            w.wait("gpsimd", "dve", w.val(("oadd", i - 2), "dve"))
                        w.dma("gql", xres[i % 2][:],
                              x_in[128 * n:128 * (n + 1), sl])
                        w.mark(("ox", i))
                        if out_proj:
                            if ("oadd", i - 2) in w.marks:
                                w.wait("tensor", "dve", w.val(("oadd", i - 2), "dve"))
                            for d2 in range(DT):
                                w.mm(p_o[i % 2][:, :], wT["Wo"](d2, n),
                                     Osb[d2][:, sl],
                                     start=(d2 == 0), stop=(d2 == DT - 1))
                            w.mark(("omm", i))
                            w.wait("vector", "pe", w.val(("omm", i), "pe"))
                            w.wait("vector", "gql", w.val(("ox", i), "gql"))
                            if ("ost", i - 2) in w.marks:
                                w.wait("vector", "gq", w.val(("ost", i - 2), "gq"))
                            w.dve("scalar_tensor_tensor", xout[i % 2][:],
                                  p_o[i % 2][:, :], bc["bo"][:, n:n + 1],
                                  xres[i % 2][:], ALU.add, ALU.add)
                        else:
                            w.wait("vector", "gql", w.val(("ox", i), "gql"))
                            if ("ost", i - 2) in w.marks:
                                w.wait("vector", "gq", w.val(("ost", i - 2), "gq"))
                            w.dve("tensor_copy", otmp[:], Osb[n][:, sl])
                            w.dve("tensor_add", xout[i % 2][:], xres[i % 2][:],
                                  otmp[:])
                        w.mark(("oadd", i))
                        w.wait("gpsimd", "dve", w.val(("oadd", i), "dve"))
                        w.dma("gq", x_out[128 * n:128 * (n + 1), sl],
                              xout[i % 2][:])
                        w.mark(("ost", i))

            wk = Walker(ctx)
            wk.run(walkO)


def emit_l0_ffn(ctx, x_in, x_out, W):
    NE = 8
    NSL = S // 128
    QG, CPG = 4, 2
    with ctx.scope() as sc:
        xn_dram = ctx.dram([D, S], BF16, "l0f_xn")
        agin = ctx.dram([1, NE], F32, "l0f_agin")
        agout = ctx.dram([NE, NE], F32, "l0f_agout")
        with ctx.scope() as sln:
            m_row = sln.sbuf([1, S], F32, "m_row")
            r_row = sln.sbuf([1, S], F32, "r_row")
            with ctx.scope() as sc2:
                emit_ln_rows(ctx, sc2, x_in, m_row, r_row)
            with ctx.scope() as sc2:
                emit_xn(ctx, sc2, x_in, m_row, r_row, xn_dram)

        trix = np.zeros((128, 128), np.float32)
        for t in range(128):
            trix[:t, t] = 1.0
        triX = ctx.const("l0trix", trix)
        Z8 = ctx.const("l0z8", np.zeros((128, NE), np.float32))
        onesrf = ctx.const("ones_rowf32", np.ones((1, 128), np.float32))
        onescf = ctx.ones_col()
        eyef = ctx.eyef32()
        eye = ctx.eye16()
        pm_sb, _ = emit_vec_load(ctx, sc, W["core_prefix"], NE, "l0pm")

        mcolsb = sc.sbuf([128, NE * NSL], BF16, "l0mcols")
        gkrow = sc.sbuf([1, S], F32, "l0gkrow")
        gk_dram = ctx.dram([1, S], F32, "l0f_gk") \
            if "l0f_gk" in DEBUG_TAPS else None

        with ctx.scope() as sr:
            rw_sb = sr.sbuf([128, NE * DT], BF16, "rw")
            xnb = [sr.sbuf([128, DT * CH], BF16, "xnb") for _ in range(2)]
            lcp = sr.sbuf([128, NE], F32, "lcp")
            mcol = sr.sbuf([128, 1], F32, "mcol")
            negm = sr.sbuf([128, 1], F32, "negm")
            esb = sr.sbuf([128, NE], F32, "esb")
            ssum = sr.sbuf([128, 1], F32, "ssum")
            dsm = sr.sbuf([128, NE], F32, "dsm")
            oh0 = sr.sbuf([128, NE], F32, "oh0")
            cumb = sr.sbuf([128, NE], F32, "cumb")
            fsel = sr.sbuf([128, NE], F32, "fsel")
            ohsb = sr.sbuf([128, NE * NSL], F32, "ohsb")
            possb = sr.sbuf([128, NE * NSL], F32, "possb")
            gsb = sr.sbuf([128, NSL], F32, "gsb")
            base = sr.sbuf([1, NE], F32, "base")
            allc = sr.sbuf([NE, NE], F32, "allc")
            offr = sr.sbuf([1, NE], F32, "offr")
            obs = sr.sbuf([128, NE], F32, "obs")
            pgb = sr.sbuf([128, NE], F32, "pgb")
            k01 = sr.sbuf([128, NE], F32, "k01")
            mselb = sr.sbuf([128, NE], F32, "mselb")

            kcol = sr.sbuf([128, 1], F32, "kcol")
            gkcol = sr.sbuf([128, 1], F32, "gkcol")
            zr8 = sr.sbuf([NE, NE], F32, "zr8")
            dbg_dram = ctx.dram([10, NE], F32, "l0f_dbg") \
                if "l0f_dbg" in DEBUG_TAPS else None
            p_l = sr.psum([128, NE], F32, "pl")
            p_pos = sr.psum([128, NE], F32, "ppos")
            p_cnt = sr.psum([1, NE], F32, "pcnt")
            p_off = sr.psum([1, NE], F32, "poff")
            p_gr = sr.psum([1, 128], F32, "pgr")
            p_ob = sr.psum([128, NE], F32, "pob")

            def walkR(w):
                w.wait("gpsimd", "gq", w.base["gq"])
                w.wait("gpsimd", "dve", w.base["dve"])
                w.wait("gpsimd", "pe", w.base["pe"])
                w.wait("gpsimd", "act", w.base["act"])
                for e in ("tensor", "vector", "scalar"):
                    w.wait(e, "ldc", w.base["ldc"])
                    w.wait(e, "dve", w.base["dve"])
                    w.wait(e, "act", w.base["act"])
                    w.wait(e, "pe", w.base["pe"])
                w.gp_memset(base[:], 0.0)
                for d in range(DT):
                    w.dma("gql", rw_sb[:, NE * d:NE * (d + 1)],
                          W["router_w"][128 * d:128 * (d + 1), :])
                w.mark("rwl")
                for c in range(NCH):
                    sl = slice(CH * c, CH * (c + 1))
                    if ("rc", c - 2) in w.marks:
                        w.wait("gpsimd", "pe", w.val(("rc", c - 2), "pe"))
                    for d in range(DT):
                        w.dma("gql", xnb[c % 2][:, d * CH:(d + 1) * CH],
                              xn_dram[128 * d:128 * (d + 1), sl])
                    w.mark(("rl", c))
                    for s4 in range(4):
                        s = 4 * c + s4
                        w.wait("tensor", "gql", w.val(("rl", c), "gql"))
                        w.wait("tensor", "gql", w.val("rwl", "gql"))
                        if ("rle", s - 1) in w.marks:
                            w.wait("tensor", "dve", w.val(("rle", s - 1), "dve"))
                        for d in range(DT):
                            w.mm(p_l[:, :],
                                 xnb[c % 2][:, d * CH + 128 * s4:
                                            d * CH + 128 * s4 + 128],
                                 rw_sb[:, NE * d:NE * (d + 1)],
                                 start=(d == 0), stop=(d == DT - 1))
                        w.mark(("rlm", s))
                        w.wait("vector", "pe", w.val(("rlm", s), "pe"))
                        if ("rex", s - 1) in w.marks:
                            w.wait("vector", "act", w.val(("rex", s - 1), "act"))
                        w.dve("tensor_copy", lcp[:], p_l[:, :], rt=True)
                        w.mark(("rle", s))
                        w.dve("reduce_max", mcol[:], lcp[:], AX.X, rt=True)
                        w.dve("tensor_scalar", negm[:], mcol[:], -1.0, None,
                              ALU.mult, ALU.bypass, rt=True)
                        w.mark(("rng", s))
                        w.wait("scalar", "dve", w.val(("rng", s), "dve"))
                        w.act(esb[:], lcp[:], AF.Exp, bias=negm[:, 0:1])
                        w.act(dsm[:], lcp[:], AF.Identity, bias=negm[:, 0:1])
                        w.mark(("rex", s))
                        w.wait("vector", "act", w.val(("rex", s), "act"))
                        w.dve("reduce_sum", ssum[:], esb[:], AX.X, rt=True)
                        w.dve("reciprocal", gsb[:, s:s + 1], ssum[:], rt=True)
                        w.dve("tensor_scalar", oh0[:], dsm[:], 0.0, None,
                              ALU.is_equal, ALU.bypass, rt=True)
                        w.dve("tensor_tensor_scan", cumb[:], oh0[:], Z8[:],
                              0.0, ALU.add, ALU.add, rt=True)
                        w.dve("tensor_scalar", fsel[:], cumb[:], 1.0, None,
                              ALU.is_equal, ALU.bypass, rt=True)
                        w.dve("tensor_mul", ohsb[:, NE * s:NE * (s + 1)],
                              oh0[:], fsel[:], rt=True)
                        w.mark(("roh", s))
                        w.wait("tensor", "dve", w.val(("roh", s), "dve"))
                        if ("rbu", s - 1) in w.marks:
                            w.wait("tensor", "dve", w.val(("rbu", s - 1), "dve"))
                        if ("rpe", s - 1) in w.marks:
                            w.wait("tensor", "dve", w.val(("rpe", s - 1), "dve"))
                        w.mm(p_pos[:, :], triX[:, :],
                             ohsb[:, NE * s:NE * (s + 1)],
                             start=True, stop=False)
                        w.mm(p_pos[:, :], onesrf[:, :], base[:, :],
                             start=False, stop=True)
                        w.mark(("rpm", s))
                        w.wait("vector", "pe", w.val(("rpm", s), "pe"))
                        w.dve("tensor_copy", possb[:, NE * s:NE * (s + 1)],
                              p_pos[:, :], rt=True)
                        w.mark(("rpe", s))
                        w.mm(p_cnt[:, :], onescf[:, :],
                             ohsb[:, NE * s:NE * (s + 1)])
                        w.mark(("rcm", s))
                        w.wait("vector", "pe", w.val(("rcm", s), "pe"))
                        w.dve("tensor_add", base[:], base[:], p_cnt[:, :],
                              rt=True)
                        w.mark(("rbu", s))
                    w.mark(("rc", c))
                w.wait("gpsimd", "dve", w.val(("rbu", NSL - 1), "dve"))
                if SIM_SINGLE_CORE:
                    w.gp_memset(zr8[:], 0.0)
                    w.dma("gq", agout[:, :], zr8[:])
                    w.mark("agz")
                    w.wait("gpsimd", "gq", w.val("agz", "gq"))
                    w.dma("gq", agout[0:1, :], base[:])
                else:
                    w.dma("gq", agin[:, :], base[:])
                w.mark("agi")
                w.wait("gpsimd", "gq", w.val("agi", "gq"))
                if not SIM_SINGLE_CORE:
                    w.gp_collective("AllGather", agout, agin)
                    w.mark("agc")
                    w.wait("gpsimd", "cc", w.val("agc", "cc"))
                w.dma("gql", allc[:, :], agout[:, :])
                w.mark("ago")
                w.wait("tensor", "gql", w.val("ago", "gql"))
                w.wait("tensor", "ldc", w.base["ldc"])
                w.mm(p_off[:, :], pm_sb[:, :], allc[:, :])
                w.mark("kof")
                w.wait("vector", "pe", w.val("kof", "pe"))
                w.dve("tensor_copy", offr[:], p_off[:, :], rt=True)
                w.mark("kor")
                w.wait("tensor", "dve", w.val("kor", "dve"))
                w.mm(p_ob[:, :], onesrf[:, :], offr[:, :])
                w.mark("kob")
                w.wait("vector", "pe", w.val("kob", "pe"))
                w.dve("tensor_copy", obs[:], p_ob[:, :], rt=True)
                w.mark("koe")
                if "l0f_dbg" in DEBUG_TAPS:
                    w.wait("gpsimd", "dve", w.val("koe", "dve"))
                    w.dma("gq", dbg_dram[0:1, 0:NE], base[:, :])
                    w.dma("gq", dbg_dram[1:2, 0:NE], offr[:, :])
                    w.dma("gq", dbg_dram[2:2 + NE, 0:NE], allc[:, :])
                for s in range(NSL):
                    w.dve("tensor_add", pgb[:],
                          possb[:, NE * s:NE * (s + 1)], obs[:], rt=True)
                    w.dve("tensor_scalar", k01[:], pgb[:], 4096.0, None,
                          ALU.is_lt, ALU.bypass, rt=True)
                    w.dve("tensor_mul", mselb[:], k01[:],
                          ohsb[:, NE * s:NE * (s + 1)], rt=True)
                    w.dve("reduce_sum", kcol[:], mselb[:], AX.X, rt=True)
                    w.dve("tensor_mul", gkcol[:], gsb[:, s:s + 1], kcol[:],
                          rt=True)
                    w.dve("tensor_copy", mcolsb[:, NE * s:NE * (s + 1)],
                          mselb[:], rt=True)
                    w.mark(("kms", s))
                    w.wait("tensor", "dve", w.val(("kms", s), "dve"))
                    if ("kgr", s - 1) in w.marks:
                        w.wait("tensor", "dve", w.val(("kgr", s - 1), "dve"))
                    w.mm(p_gr[:, :], gkcol[:, :], eyef[:, :])
                    w.mark(("kgm", s))
                    w.wait("vector", "pe", w.val(("kgm", s), "pe"))
                    w.dve("tensor_copy", gkrow[0:1, 128 * s:128 * (s + 1)],
                          p_gr[:, :], rt=True)
                    w.mark(("kgr", s))
                if "l0f_gk" in DEBUG_TAPS:
                    w.wait("gpsimd", "dve", w.val(("kgr", NSL - 1), "dve"))
                    w.dma("gq", gk_dram[0:1, :], gkrow[:, :])

            wk = Walker(ctx)
            wk.run(walkR)

        w1_2d = W["w1"].rearrange("e d f -> (e d) f")
        w2_2d = W["w2"].rearrange("e f d -> (e f) d")
        NHT = 16
        NPH = 2 * NE * QG
        with ctx.scope() as se:
            w1sb = se.sbuf([128, DT * 2048], BF16, "w1sb")
            w2sb = se.sbuf([128, NHT * 1024], BF16, "w2sb")
            xng = [se.sbuf([128, DT * CH], BF16, "xng") for _ in range(CPG)]
            xe16 = [se.sbuf([128, DT * CH], BF16, "xe16") for _ in range(CPG)]
            h16 = se.sbuf([128, NHT * CH], BF16, "h16")
            yac = [se.sbuf([128, CPG * CH], F32, "yac") for _ in range(DT)]
            meb16 = se.sbuf([128, CH], BF16, "meb16")
            mrowall = [se.sbuf([1, NE * CH], BF16, "mrowall")
                       for _ in range(CPG)]
            xres = [se.sbuf([128, CH], F32, "xres") for _ in range(2)]
            xoutb = [se.sbuf([128, CH], F32, "xoutb") for _ in range(2)]
            tmul = se.sbuf([128, CH], F32, "tmul")
            p_h = [se.psum([128, CH], F32, "ph") for _ in range(2)]
            p_y = [se.psum([128, CH], F32, "py") for _ in range(2)]
            p_mb = se.psum([128, CH], F32, "pmb")
            p_gk = se.psum([128, CH], F32, "pgk")
            p_row = [se.psum([1, 128], BF16, "prow") for _ in range(2)]

            def ldw1h(w, ph):
                k, half = ph // 2, ph % 2
                e = k % NE
                for r in range(DT):
                    w.dma("gql", w1sb[:, r * 2048:(r + 1) * 2048],
                          w1_2d[e * 1024 + 128 * r:e * 1024 + 128 * (r + 1),
                                half * 2048:(half + 1) * 2048])
                w.mark(("ew1", ph))

            def ldw2h(w, ph):
                k, half = ph // 2, ph % 2
                e = k % NE
                for r in range(NHT):
                    w.dma("gql", w2sb[:, r * 1024:(r + 1) * 1024],
                          w2_2d[e * 4096 + half * 2048 + 128 * r:
                                e * 4096 + half * 2048 + 128 * (r + 1), :])
                w.mark(("ew2", ph))

            def ldxng(w, qg):
                for cc in range(CPG):
                    c = CPG * qg + cc
                    for d in range(DT):
                        w.dma("gql", xng[cc][:, d * CH:(d + 1) * CH],
                              xn_dram[128 * d:128 * (d + 1),
                                      CH * c:CH * (c + 1)])
                w.mark(("exng", qg))

            def walkE(w):
                w.wait("gpsimd", "gq", w.base["gq"])
                w.wait("gpsimd", "dve", w.base["dve"])
                w.wait("gpsimd", "pe", w.base["pe"])
                w.wait("gpsimd", "act", w.base["act"])
                for e in ("tensor", "vector", "scalar"):
                    w.wait(e, "ldc", w.base["ldc"])
                    w.wait(e, "dve", w.base["dve"])
                    w.wait(e, "act", w.base["act"])
                    w.wait(e, "pe", w.base["pe"])
                ldw1h(w, 0)
                ldw2h(w, 0)
                ldxng(w, 0)
                seq = []
                jj = 0
                for k in range(NE * QG):
                    qg, e = k // NE, k % NE
                    if e == 0:
                        for cc2 in range(CPG):
                            for e2 in range(NE):
                                for s4 in range(4):
                                    s = 4 * (CPG * qg + cc2) + s4
                                    if ("mre", jj - 2) in w.marks:
                                        w.wait("tensor", "dve",
                                               w.val(("mre", jj - 2), "dve"))
                                    w.tp(p_row[jj % 2][:, :],
                                         mcolsb[:, NE * s + e2:
                                                NE * s + e2 + 1], eye[:, :])
                                    w.mark(("mrt", jj))
                                    w.wait("vector", "pe",
                                           w.val(("mrt", jj), "pe"))
                                    if ("emb", (NE * qg - 1, CPG - 1)) in w.marks:
                                        w.wait("vector", "pe",
                                               w.val(("emb", (NE * qg - 1,
                                                              CPG - 1)), "pe"))
                                    w.dve("tensor_copy",
                                          mrowall[cc2][0:1,
                                                       CH * e2 + 128 * s4:
                                                       CH * e2 + 128 * (s4 + 1)],
                                          p_row[jj % 2][:, :])
                                    w.mark(("mre", jj))
                                    jj += 1
                                w.mark(("mrow", (qg, cc2, e2)))
                    for half in range(2):
                        ph = 2 * k + half
                        for cc in range(CPG):
                            c = CPG * qg + cc
                            sl = slice(CH * c, CH * (c + 1))
                            ccsl = slice(CH * cc, CH * (cc + 1))
                            prevs = seq[-1] if seq else None
                            if half == 0:
                                if ("eme", (k, cc - 1)) in w.marks:
                                    w.wait("tensor", "dve",
                                           w.val(("eme", (k, cc - 1)), "dve"))
                                elif ("eme", (k - 1, CPG - 1)) in w.marks:
                                    w.wait("tensor", "dve",
                                           w.val(("eme", (k - 1, CPG - 1)),
                                                 "dve"))
                                w.wait("tensor", "dve",
                                       w.val(("mrow", (qg, cc, e)), "dve"))
                                w.mm(p_mb[:, :], ctx.ones_row16()[:, :],
                                     mrowall[cc][0:1, CH * e:CH * (e + 1)])
                                w.mark(("emb", (k, cc)))
                                w.wait("vector", "pe",
                                       w.val(("emb", (k, cc)), "pe"))
                                w.dve("tensor_copy", meb16[:], p_mb[:, :],
                                      rt=True)
                                w.mark(("eme", (k, cc)))
                                w.wait("vector", "gql",
                                       w.val(("exng", qg), "gql"))
                                if ("ehm", (k - 1, 1, cc)) in w.marks:
                                    w.wait("vector", "pe",
                                           w.val(("ehm", (k - 1, 1, cc)),
                                                 "pe"))
                                for d in range(DT):
                                    w.dve("tensor_mul",
                                          xe16[cc][:, d * CH:(d + 1) * CH],
                                          xng[cc][:, d * CH:(d + 1) * CH],
                                          meb16[:])
                                w.mark(("exe", (k, cc)))
                            w.wait("tensor", "dve",
                                   w.val(("exe", (k, cc)), "dve"))
                            w.wait("tensor", "gql", w.val(("ew1", ph), "gql"))
                            for ht in range(NHT):
                                if ("ehg", (k, half, cc, ht - 2)) in w.marks:
                                    w.wait("tensor", "act",
                                           w.val(("ehg", (k, half, cc,
                                                          ht - 2)), "act"))
                                elif prevs and ("ehg", prevs + (NHT - 2 + ht,)) in w.marks:
                                    w.wait("tensor", "act",
                                           w.val(("ehg",
                                                  prevs + (NHT - 2 + ht,)),
                                                 "act"))
                                for d in range(DT):
                                    w.mm(p_h[ht % 2][:, :],
                                         w1sb[:, d * 2048 + 128 * ht:
                                              d * 2048 + 128 * (ht + 1)],
                                         xe16[cc][:, d * CH:(d + 1) * CH],
                                         start=(d == 0), stop=(d == DT - 1))
                                w.mark(("ehm2", (k, half, cc, ht)))
                                w.wait("scalar", "pe",
                                       w.val(("ehm2", (k, half, cc, ht)),
                                             "pe"))
                                if ht == 0 and prevs and \
                                        ("eym", prevs + (DT - 1,)) in w.marks:
                                    w.wait("scalar", "pe",
                                           w.val(("eym", prevs + (DT - 1,)),
                                                 "pe"))
                                w.act(h16[:, ht * CH:(ht + 1) * CH],
                                      p_h[ht % 2][:, :], AF.Gelu_apprx_tanh)
                                w.mark(("ehg", (k, half, cc, ht)))
                            w.mark(("ehm", (k, half, cc)))
                            if cc == CPG - 1 and ph + 1 < NPH:
                                w.wait("gpsimd", "pe",
                                       w.val(("ehm", (k, half, cc)), "pe"))
                                ldw1h(w, ph + 1)
                            w.wait("tensor", "act",
                                   w.val(("ehg", (k, half, cc, NHT - 1)),
                                         "act"))
                            w.wait("tensor", "gql", w.val(("ew2", ph), "gql"))
                            first = (e == 0 and half == 0)
                            last = (e == NE - 1 and half == 1)
                            for d in range(DT):
                                if ("eyv", (k, half, cc, d - 2)) in w.marks:
                                    w.wait("tensor", "dve",
                                           w.val(("eyv", (k, half, cc,
                                                          d - 2)), "dve"))
                                elif prevs and \
                                        ("eyv", prevs + (DT - 2 + d,)) in w.marks:
                                    w.wait("tensor", "dve",
                                           w.val(("eyv",
                                                  prevs + (DT - 2 + d,)),
                                                 "dve"))
                                for ht in range(NHT):
                                    w.mm(p_y[d % 2][:, :],
                                         w2sb[:, ht * 1024 + 128 * d:
                                              ht * 1024 + 128 * (d + 1)],
                                         h16[:, ht * CH:(ht + 1) * CH],
                                         start=(ht == 0),
                                         stop=(ht == NHT - 1))
                                w.mark(("eym", (k, half, cc, d)))
                                w.wait("vector", "pe",
                                       w.val(("eym", (k, half, cc, d)),
                                             "pe"))
                                if first:
                                    w.dve("tensor_copy", yac[d][:, ccsl],
                                          p_y[d % 2][:, :], rt=last)
                                else:
                                    w.dve("tensor_add", yac[d][:, ccsl],
                                          yac[d][:, ccsl], p_y[d % 2][:, :],
                                          rt=last)
                                w.mark(("eyv", (k, half, cc, d)))
                            seq.append((k, half, cc))
                        if ph + 1 < NPH:
                            w.wait("gpsimd", "pe",
                                   w.val(("eym", (k, half, CPG - 1, DT - 1)),
                                         "pe"))
                            ldw2h(w, ph + 1)
                    if e == NE - 1 and qg + 1 < QG:
                        w.wait("gpsimd", "dve",
                               w.val(("exe", (k, CPG - 1)), "dve"))
                        ldxng(w, qg + 1)
                    if e == NE - 1:
                        for cc in range(CPG):
                            c = CPG * qg + cc
                            sl = slice(CH * c, CH * (c + 1))
                            ccsl = slice(CH * cc, CH * (cc + 1))
                            if ("efm", (qg, cc - 1)) in w.marks:
                                w.wait("tensor", "dve",
                                       w.val(("efm", (qg, cc - 1)), "dve"))
                            elif ("efm", (qg - 1, CPG - 1)) in w.marks:
                                w.wait("tensor", "dve",
                                       w.val(("efm", (qg - 1, CPG - 1)),
                                             "dve"))
                            w.wait("tensor", "dve",
                                   w.val(("eyv", (k, 1, cc, DT - 1)), "dve"))
                            w.mm(p_gk[:, :], onesrf[:, :], gkrow[0:1, sl])
                            w.mark(("egk", (qg, cc)))
                            for d in range(DT):
                                i = CPG * DT * qg + DT * cc + d
                                if ("eos", i - 2) in w.marks:
                                    w.wait("gpsimd", "dve",
                                           w.val(("eos", i - 2), "dve"))
                                w.dma("gql", xres[d % 2][:],
                                      x_in[128 * d:128 * (d + 1), sl])
                                w.mark(("exr", i))
                                w.wait("vector", "pe",
                                       w.val(("egk", (qg, cc)), "pe"))
                                w.wait("vector", "gql",
                                       w.val(("exr", i), "gql"))
                                if ("eos", i - 2) in w.marks:
                                    w.wait("vector", "gq",
                                           w.val(("eos", i - 2), "gq"))
                                w.dve("tensor_mul", tmul[:],
                                      yac[d][:, ccsl], p_gk[:, :], rt=True)
                                w.dve("tensor_add", xoutb[d % 2][:],
                                      tmul[:], xres[d % 2][:])
                                w.mark(("eoa", i))
                                w.wait("gpsimd", "dve",
                                       w.val(("eoa", i), "dve"))
                                w.dma("gq",
                                      x_out[128 * d:128 * (d + 1), sl],
                                      xoutb[d % 2][:])
                                w.mark(("eos", i))
                            w.mark(("efm", (qg, cc)))

            wk = Walker(ctx)
            wk.run(walkE)


def _tri_keep():
    m = np.zeros((128, 128), BF)
    for i in range(128):
        m[i, i:] = 1.0
    return m


def emit_l3_attn(ctx, x_in, x_out, W):
    NH = 16
    with ctx.scope() as sc:
        xn_dram = ctx.dram([D, S], BF16, "l3a_xn")
        qf_dram = ctx.dram([D, S], BF16, "l3a_qf")
        kf_dram = ctx.dram([D, S], BF16, "l3a_kf")
        v_dram = ctx.dram([D, S], BF16, "l3a_v")
        with ctx.scope() as sln:
            m_row = sln.sbuf([1, S], F32, "m_row")
            r_row = sln.sbuf([1, S], F32, "r_row")
            with ctx.scope() as sc2:
                emit_ln_rows(ctx, sc2, x_in, m_row, r_row)
            with ctx.scope() as sc2:
                emit_xn(ctx, sc2, x_in, m_row, r_row, xn_dram)

        _, wout_t, _ = emit_load_w_nat(ctx, sc, W["Wout"], 1024, 1024, "wout")
        bco = emit_load_bias_cols(ctx, sc, W["bout"], 1024, "bout")
        Osb = [sc.sbuf([128, S], BF16, "osb") for _ in range(DT)]

        e2 = np.zeros((128, 2), np.float32)
        e2[:64, 0] = 1.0
        e2[64:, 1] = 1.0
        E2f = ctx.const("e2cols", e2)
        E2T = ctx.const("e2rows", np.ascontiguousarray(e2.T))
        tri = ctx.const("l3tri", _tri_keep())
        onesrf = ctx.const("ones_rowf32", np.ones((1, 128), np.float32))
        epsc = ctx.const("eps24col", np.full((128, 1), 1e-24, np.float32))
        eye = ctx.eye16()
        eyef = ctx.eyef32()

        qn_sb, _ = emit_vec_load(ctx, sc, W["quad_nodes"], 2, "qnod")
        qw_sb, _ = emit_vec_load(ctx, sc, W["quad_weights"], 2, "qwts")
        om2d = W["omega"].rearrange("r h d m -> (r h d) m")

        with ctx.scope() as sf:
            _, wq_t, _ = emit_load_w_nat(ctx, sf, W["Wqkv"], 1024, 3072,
                                         "wqkv")
            bcq = emit_load_bias_cols(ctx, sf, W["bqkv"], 3072, "bqkv")
            om_sb = sf.sbuf([128, 2 * NH * 32], BF16, "om")
            qc = sf.sbuf([128, 6], F32, "qc")
            prow = sf.sbuf([1, 6], F32, "prow")
            srow = sf.sbuf([1, 2], F32, "srow")
            wrow = sf.sbuf([1, 2], F32, "wrow")
            ta = sf.sbuf([1, 2], F32, "ta")
            tb = sf.sbuf([1, 2], F32, "tb")
            tc = sf.sbuf([1, 2], F32, "tc")
            xnb = [sf.sbuf([128, DT * CH], BF16, "xnb") for _ in range(2)]
            qb = [sf.sbuf([128, CH], F32, "qb") for _ in range(2)]
            sqb = [sf.sbuf([128, CH], F32, "sqb") for _ in range(2)]
            srootb = [sf.sbuf([2, CH], F32, "sroot") for _ in range(2)]
            rrb = [sf.sbuf([2, CH], F32, "rrb") for _ in range(2)]
            qn16 = [sf.sbuf([128, CH], BF16, "qn16") for _ in range(2)]
            usb = [sf.sbuf([32, CH], F32, "usb") for _ in range(2)]
            uclb = [sf.sbuf([32, CH], F32, "uclb") for _ in range(2)]
            feb = [sf.sbuf([32, CH], F32, "feb") for _ in range(2)]
            f16b = [sf.sbuf([128, CH], BF16, "f16b") for _ in range(2)]
            v16b = [sf.sbuf([128, CH], BF16, "v16b") for _ in range(2)]
            p_p = [sf.psum([128, CH], F32, "pp") for _ in range(2)]
            p_f = [sf.psum([32, CH], F32, "pf") for _ in range(2)]
            p_b = sf.psum([128, CH], F32, "pb")
            p_s = sf.psum([2, CH], F32, "ps")
            p_r = sf.psum([1, 8], F32, "pr")
            p_q = sf.psum([128, 8], F32, "pq")

            def walkF(w):
                for e in ("gpsimd",):
                    w.wait(e, "gq", w.base["gq"])
                    w.wait(e, "dve", w.base["dve"])
                    w.wait(e, "pe", w.base["pe"])
                    w.wait(e, "act", w.base["act"])
                for e in ("tensor", "vector", "scalar"):
                    w.wait(e, "ldc", w.base["ldc"])
                    w.wait(e, "dve", w.base["dve"])
                    w.wait(e, "act", w.base["act"])
                    w.wait(e, "pe", w.base["pe"])
                for rh in range(2 * NH):
                    w.dma("gql", om_sb[0:64, rh * 32:rh * 32 + 32],
                          om2d[rh * 64:rh * 64 + 64, :])
                    w.dma("gql", om_sb[64:128, rh * 32:rh * 32 + 32],
                          om2d[rh * 64:rh * 64 + 64, :])
                w.mark("oml")
                w.mm(p_r[0:1, 0:2], qn_sb[:, :], eyef[0:2, 0:2])
                w.mark("pm1")
                w.wait("vector", "pe", w.val("pm1", "pe"))
                w.dve("tensor_copy", srow[:], p_r[0:1, 0:2], rt=True)
                w.mark("pd1")
                w.wait("tensor", "dve", w.val("pd1", "dve"))
                w.mm(p_r[0:1, 0:2], qw_sb[:, :], eyef[0:2, 0:2])
                w.mark("pm2")
                w.wait("vector", "pe", w.val("pm2", "pe"))
                w.dve("tensor_copy", wrow[:], p_r[0:1, 0:2], rt=True)
                w.dve("tensor_scalar", prow[0:1, 2:4], srow[:], -1.0, None,
                      ALU.mult, ALU.bypass, rt=True)
                w.mark("pd2")
                w.wait("scalar", "dve", w.val("pd2", "dve"))
                w.act(ta[:], srow[:], AF.Relu)
                w.mark("pa1")
                w.wait("vector", "act", w.val("pa1", "act"))
                w.dve("tensor_scalar", tb[:], ta[:], 2.0, None, ALU.mult,
                      ALU.bypass, rt=True)
                w.mark("pd3")
                w.wait("scalar", "dve", w.val("pd3", "dve"))
                w.act(prow[0:1, 0:2], tb[:], AF.Sqrt)
                w.act(ta[:], wrow[:], AF.Relu)
                w.act(tc[:], ta[:], AF.Sqrt)
                w.mark("pa2")
                w.wait("vector", "act", w.val("pa2", "act"))
                w.dve("tensor_scalar", prow[0:1, 4:6], tc[:],
                      float(1.0 / np.sqrt(32.0)), None, ALU.mult, ALU.bypass,
                      rt=True)
                w.mark("pd4")
                w.wait("tensor", "dve", w.val("pd4", "dve"))
                w.wait("tensor", "act", w.val("pa2", "act"))
                w.mm(p_q[:, 0:6], onesrf[:, :], prow[:, :])
                w.mark("pm3")
                w.wait("vector", "pe", w.val("pm3", "pe"))
                w.dve("tensor_copy", qc[:, 0:6], p_q[:, 0:6], rt=True)
                w.mark("prep")

                qk_hist = []
                for c in range(NCH):
                    sl = slice(CH * c, CH * (c + 1))
                    if ("fc", c - 2) in w.marks:
                        w.wait("gpsimd", "pe", w.val(("fc", c - 2), "pe"))
                    for d in range(DT):
                        w.dma("gql", xnb[c % 2][:, d * CH:(d + 1) * CH],
                              xn_dram[128 * d:128 * (d + 1), sl])
                    w.mark(("fl", c))
                    for n in range(24):
                        g = c * 24 + n
                        kind = n // 8
                        w.wait("tensor", "gql", w.val(("fl", c), "gql"))
                        w.wait("tensor", "dve", w.val("prep", "dve"))
                        if ("ev", g - 2) in w.marks:
                            w.wait("tensor", "dve", w.val(("ev", g - 2), "dve"))
                        for d2 in range(DT):
                            w.mm(p_p[g % 2][:, :], wq_t(d2, n),
                                 xnb[c % 2][:, d2 * CH:(d2 + 1) * CH],
                                 start=(d2 == 0), stop=(d2 == DT - 1))
                        w.mark(("fm", g))
                        if kind == 2:
                            w.wait("vector", "pe", w.val(("fm", g), "pe"))
                            if ("vst", g - 2) in w.marks:
                                w.wait("vector", "gq",
                                       w.val(("vst", g - 2), "gq"))
                            w.dve("tensor_scalar", v16b[g % 2][:],
                                  p_p[g % 2][:, :], bcq[:, n:n + 1], None,
                                  ALU.add, ALU.bypass)
                            w.mark(("ev", g))
                            w.wait("gpsimd", "dve", w.val(("ev", g), "dve"))
                            w.dma("gq",
                                  v_dram[128 * (n - 16):128 * (n - 15), sl],
                                  v16b[g % 2][:])
                            w.mark(("vst", g))
                            continue
                        gp1 = qk_hist[-1] if qk_hist else None
                        gp2 = qk_hist[-2] if len(qk_hist) >= 2 else None
                        pi = len(qk_hist) % 2
                        w.wait("vector", "pe", w.val(("fm", g), "pe"))
                        w.dve("tensor_scalar", qb[pi][:], p_p[g % 2][:, :],
                              bcq[:, n:n + 1], None, ALU.add, ALU.bypass,
                              rt=True)
                        w.mark(("ev", g))
                        if gp2 is not None:
                            w.wait("vector", "pe", w.val(("sm", gp2), "pe"))
                        w.dve("tensor_mul", sqb[pi][:], qb[pi][:], qb[pi][:])
                        w.mark(("sq", g))
                        w.wait("tensor", "dve", w.val(("sq", g), "dve"))
                        if gp1 is not None:
                            w.wait("tensor", "act", w.val(("sr", gp1), "act"))
                        w.mm(p_s[:, :], E2f[:, :], sqb[pi][:])
                        w.mark(("sm", g))
                        w.wait("scalar", "pe", w.val(("sm", g), "pe"))
                        if gp2 is not None:
                            w.wait("scalar", "dve", w.val(("rr", gp2), "dve"))
                        w.act(srootb[pi][:], p_s[:, :], AF.Sqrt,
                              bias=epsc[0:2, 0:1])
                        w.mark(("sr", g))
                        w.wait("vector", "act", w.val(("sr", g), "act"))
                        if gp2 is not None:
                            w.wait("vector", "pe", w.val(("bm", gp2), "pe"))
                        w.dve("reciprocal", rrb[pi][:], srootb[pi][:])
                        w.mark(("rr", g))
                        w.wait("tensor", "dve", w.val(("rr", g), "dve"))
                        if gp1 is not None:
                            w.wait("tensor", "dve", w.val(("qn", gp1), "dve"))
                        w.mm(p_b[:, :], E2T[:, :], rrb[pi][:])
                        w.mark(("bm", g))
                        w.wait("vector", "pe", w.val(("bm", g), "pe"))
                        if gp2 is not None:
                            w.wait("vector", "pe",
                                   w.val(("ff", (gp2, 3)), "pe"))
                        w.dve("tensor_mul", qn16[pi][:], qb[pi][:], p_b[:, :])
                        w.mark(("qn", g))
                        for i in range(2):
                            for r in range(2):
                                j = 2 * i + r
                                h = 2 * (n % 8) + i
                                w.wait("tensor", "dve", w.val(("qn", g), "dve"))
                                w.wait("tensor", "gql", w.val("oml", "gql"))
                                if j >= 2:
                                    w.wait("tensor", "act",
                                           w.val(("fu", (g, j - 2)), "act"))
                                elif gp1 is not None:
                                    w.wait("tensor", "act",
                                           w.val(("fu", (gp1, j + 2)), "act"))
                                w.mm(p_f[j % 2][:, :],
                                     om_sb[64 * i:64 * i + 64,
                                           (r * NH + h) * 32:(r * NH + h) * 32 + 32],
                                     qn16[pi][64 * i:64 * i + 64, :])
                                w.mark(("ff", (g, j)))
                                w.wait("scalar", "pe", w.val(("ff", (g, j)), "pe"))
                                if j >= 2:
                                    w.wait("scalar", "dve",
                                           w.val(("fc2", (g, j - 2)), "dve"))
                                elif gp1 is not None:
                                    w.wait("scalar", "dve",
                                           w.val(("fc2", (gp1, j + 2)), "dve"))
                                w.act(usb[j % 2][:], p_f[j % 2][:, :],
                                      AF.Identity, scale=qc[0:32, r:r + 1],
                                      bias=qc[0:32, 2 + r:3 + r])
                                w.mark(("fu", (g, j)))
                                w.wait("vector", "act", w.val(("fu", (g, j)), "act"))
                                if j >= 2:
                                    w.wait("vector", "act",
                                           w.val(("fe", (g, j - 2)), "act"))
                                elif gp1 is not None:
                                    w.wait("vector", "act",
                                           w.val(("fe", (gp1, j + 2)), "act"))
                                w.dve("tensor_scalar", uclb[j % 2][:],
                                      usb[j % 2][:], 20.0, -20.0,
                                      ALU.min, ALU.max)
                                w.mark(("fc2", (g, j)))
                                w.wait("scalar", "dve", w.val(("fc2", (g, j)), "dve"))
                                w.act(feb[j % 2][:], uclb[j % 2][:], AF.Exp)
                                w.mark(("fe", (g, j)))
                                if j == 0 and gp2 is not None:
                                    w.wait("scalar", "gq",
                                           w.val(("fst", gp2), "gq"))
                                w.act(f16b[pi][64 * i + 32 * r:
                                              64 * i + 32 * r + 32, :],
                                      feb[j % 2][:], AF.Identity,
                                      scale=qc[0:32, 4 + r:5 + r])
                                w.mark(("fs", (g, j)))
                        w.wait("gpsimd", "act", w.val(("fs", (g, 3)), "act"))
                        dst = qf_dram if kind == 0 else kf_dram
                        w.dma("gq", dst[128 * (n % 8):128 * (n % 8) + 128, sl],
                              f16b[pi][:])
                        w.mark(("fst", g))
                        qk_hist.append(g)
                    w.mark(("fc", c))

            wk = Walker(ctx)
            wk.run(walkF)

        with ctx.scope() as sa:
            qfh = [sa.sbuf([64, S], BF16, "qfh") for _ in range(2)]
            kfh = [sa.sbuf([64, S], BF16, "kfh") for _ in range(2)]
            vh = [sa.sbuf([64, S], BF16, "vh") for _ in range(2)]
            vaug = sa.sbuf([128, 4 * 65], BF16, "vaug")
            kftm = sa.sbuf([128, 4 * 64], BF16, "kftm")
            A16 = [sa.sbuf([128, CH], BF16, "a16") for _ in range(4)]
            state = sa.sbuf([64, 65], F32, "state")
            st16 = sa.sbuf([64, 65], BF16, "st16")
            mxr = sa.sbuf([1, CH], F32, "mxr")
            rl = sa.sbuf([1, CH], F32, "rl")
            rbs = sa.sbuf([64, CH], F32, "rbs")
            p_t = [sa.psum([128, 64], BF16, "pt") for _ in range(2)]
            pA = [sa.psum([128, CH], F32, "pA") for _ in range(2)]
            pC = sa.psum([65, CH], F32, "pC")
            pU = sa.psum([64, 65], F32, "pU")
            pR = sa.psum([64, CH], F32, "pR")

            def walkA(w):
                w.wait("gpsimd", "gq", w.base["gq"])
                w.wait("gpsimd", "dve", w.base["dve"])
                w.wait("gpsimd", "pe", w.base["pe"])
                w.wait("gpsimd", "act", w.base["act"])
                w.wait("tensor", "dve", w.base["dve"])
                w.wait("tensor", "ldc", w.base["ldc"])
                w.wait("vector", "pe", w.base["pe"])
                w.wait("vector", "act", w.base["act"])
                w.gp_memset(vaug[:], 1.0)
                for h in range(NH):
                    if ("hdone", h - 2) in w.marks:
                        w.wait("gpsimd", "pe", w.val(("hdone", h - 2), "pe"))
                    if ("hdone", h - 1) in w.marks:
                        w.wait("gpsimd", "dve", w.val(("hdone", h - 1), "dve"))
                    w.gp_memset(state[:], 0.0)
                    w.dma("gql", qfh[h % 2][:], qf_dram[64 * h:64 * h + 64, :])
                    w.dma("gql", kfh[h % 2][:], kf_dram[64 * h:64 * h + 64, :])
                    w.dma("gql", vh[h % 2][:], v_dram[64 * h:64 * h + 64, :])
                    w.mark(("hl", h))
                    for c in range(NCH):
                        kc = (h, c)
                        pv = (h, c - 1) if c > 0 else \
                            ((h - 1, NCH - 1) if h > 0 else None)
                        bt = CH * c
                        for kq in range(4):
                            for which in range(2):
                                tt = 2 * kq + which
                                w.wait("tensor", "gql", w.val(("hl", h), "gql"))
                                if ("aev", (kc, tt - 2)) in w.marks:
                                    w.wait("tensor", "dve",
                                           w.val(("aev", (kc, tt - 2)), "dve"))
                                elif pv and ("aev", (pv, tt + 6)) in w.marks:
                                    w.wait("tensor", "dve",
                                           w.val(("aev", (pv, tt + 6)), "dve"))
                                src = vh[h % 2] if which == 0 else kfh[h % 2]
                                w.tp(p_t[tt % 2][:, :],
                                     src[:, bt + 128 * kq:bt + 128 * (kq + 1)],
                                     eye[0:64, 0:64])
                                w.mark(("atp", (kc, tt)))
                                w.wait("vector", "pe",
                                       w.val(("atp", (kc, tt)), "pe"))
                                if pv and ("aupd", pv) in w.marks:
                                    w.wait("vector", "pe",
                                           w.val(("aupd", pv), "pe"))
                                if which == 0:
                                    w.dve("tensor_copy",
                                          vaug[:, 65 * kq:65 * kq + 64],
                                          p_t[tt % 2][:, :])
                                else:
                                    w.dve("tensor_copy",
                                          kftm[:, 64 * kq:64 * (kq + 1)],
                                          p_t[tt % 2][:, :])
                                w.mark(("aev", (kc, tt)))
                        for kq in range(4):
                            if ("aev2", (kc, kq - 2)) in w.marks:
                                w.wait("tensor", "dve",
                                       w.val(("aev2", (kc, kq - 2)), "dve"))
                            elif pv and ("aev2", (pv, kq + 2)) in w.marks:
                                w.wait("tensor", "dve",
                                       w.val(("aev2", (pv, kq + 2)), "dve"))
                            w.mm(pA[kq % 2][:, 0:CH - 128 * kq],
                                 kfh[h % 2][:, bt + 128 * kq:bt + 128 * (kq + 1)],
                                 qfh[h % 2][:, bt + 128 * kq:bt + CH])
                            w.mark(("amm", (kc, kq)))
                            w.wait("vector", "pe", w.val(("amm", (kc, kq)), "pe"))
                            if pv and ("actx", pv) in w.marks:
                                w.wait("vector", "pe", w.val(("actx", pv), "pe"))
                            w.dve("tensor_mul",
                                  A16[kq][:, 128 * kq:128 * (kq + 1)],
                                  pA[kq % 2][:, 0:128], tri[:, :])
                            if kq < 3:
                                w.dve("tensor_copy",
                                      A16[kq][:, 128 * (kq + 1):CH],
                                      pA[kq % 2][:, 128:CH - 128 * kq])
                            w.mark(("aev2", (kc, kq)))
                        w.wait("vector", "gql", w.val(("hl", h), "gql"))
                        if pv and ("actx", pv) in w.marks:
                            w.wait("vector", "pe", w.val(("actx", pv), "pe"))
                        w.dve("tensor_copy", st16[:], state[:], rt=True)
                        w.mark(("as16", kc))
                        w.wait("tensor", "dve", w.val(("as16", kc), "dve"))
                        w.wait("tensor", "dve", w.val(("aev2", (kc, 3)), "dve"))
                        w.wait("tensor", "dve", w.val(("aev", (kc, 7)), "dve"))
                        if pv and ("afin", pv) in w.marks:
                            w.wait("tensor", "dve", w.val(("afin", pv), "dve"))
                        w.mm(pC[:, :], st16[:], qfh[h % 2][:, bt:bt + CH],
                             start=True, stop=False)
                        for kq in (3, 2, 1):
                            w.mm(pC[:, 128 * kq:CH],
                                 vaug[:, 65 * kq:65 * kq + 65],
                                 A16[kq][:, 128 * kq:CH],
                                 start=False, stop=False)
                        w.mm(pC[:, :], vaug[:, 0:65], A16[0][:, :],
                             start=False, stop=True)
                        w.mark(("actx", kc))
                        if pv and ("aadd", pv) in w.marks:
                            w.wait("tensor", "dve", w.val(("aadd", pv), "dve"))
                        for kq in range(4):
                            w.mm(pU[:, :], kftm[:, 64 * kq:64 * (kq + 1)],
                                 vaug[:, 65 * kq:65 * kq + 65],
                                 start=(kq == 0), stop=(kq == 3))
                        w.mark(("aupd", kc))
                        w.wait("vector", "pe", w.val(("aupd", kc), "pe"))
                        w.dve("tensor_add", state[:], state[:], pU[:, :],
                              rt=True)
                        w.mark(("aadd", kc))
                        w.wait("vector", "pe", w.val(("actx", kc), "pe"))
                        if pv and ("arbm", pv) in w.marks:
                            w.wait("vector", "pe", w.val(("arbm", pv), "pe"))
                        w.dve("tensor_scalar", mxr[:], pC[64:65, :], 1e-6,
                              None, ALU.max, ALU.bypass, rt=True)
                        w.dve("reciprocal", rl[:], mxr[:])
                        w.mark(("arec", kc))
                        w.wait("tensor", "dve", w.val(("arec", kc), "dve"))
                        if pv and ("arb", pv) in w.marks:
                            w.wait("tensor", "dve", w.val(("arb", pv), "dve"))
                        w.mm(pR[:, :], onesrf[0:1, 0:64], rl[:])
                        w.mark(("arbm", kc))
                        w.wait("vector", "pe", w.val(("arbm", kc), "pe"))
                        w.dve("tensor_copy", rbs[:], pR[:, :], rt=True)
                        w.mark(("arb", kc))
                        hr = (h % 2) * 64
                        w.dve("tensor_mul",
                              Osb[h // 2][hr:hr + 64, bt:bt + CH],
                              pC[0:64, :], rbs[:])
                        w.mark(("afin", kc))
                    w.mark(("hdone", h))

            wk = Walker(ctx)
            wk.run(walkA)

        with ctx.scope() as so:
            xres = [so.sbuf([128, CH], F32, "xres") for _ in range(2)]
            xoutb = [so.sbuf([128, CH], F32, "xout") for _ in range(2)]
            p_o = [so.psum([128, CH], F32, "po") for _ in range(2)]

            def walkO(w):
                w.wait("gpsimd", "gq", w.base["gq"])
                w.wait("gpsimd", "dve", w.base["dve"])
                w.wait("gpsimd", "pe", w.base["pe"])
                w.wait("tensor", "dve", w.base["dve"])
                w.wait("tensor", "ldc", w.base["ldc"])
                for c in range(NCH):
                    sl = slice(CH * c, CH * (c + 1))
                    for n in range(DT):
                        i = c * DT + n
                        if ("oadd", i - 2) in w.marks:
                            w.wait("gpsimd", "dve", w.val(("oadd", i - 2), "dve"))
                        w.dma("gql", xres[i % 2][:],
                              x_in[128 * n:128 * (n + 1), sl])
                        w.mark(("ox", i))
                        if ("oadd", i - 2) in w.marks:
                            w.wait("tensor", "dve", w.val(("oadd", i - 2), "dve"))
                        for d2 in range(DT):
                            w.mm(p_o[i % 2][:, :], wout_t(d2, n),
                                 Osb[d2][:, sl],
                                 start=(d2 == 0), stop=(d2 == DT - 1))
                        w.mark(("omm", i))
                        w.wait("vector", "pe", w.val(("omm", i), "pe"))
                        w.wait("vector", "gql", w.val(("ox", i), "gql"))
                        if ("ost", i - 2) in w.marks:
                            w.wait("vector", "gq", w.val(("ost", i - 2), "gq"))
                        w.dve("scalar_tensor_tensor", xoutb[i % 2][:],
                              p_o[i % 2][:, :], bco[:, n:n + 1],
                              xres[i % 2][:], ALU.add, ALU.add)
                        w.mark(("oadd", i))
                        w.wait("gpsimd", "dve", w.val(("oadd", i), "dve"))
                        w.dma("gq", x_out[128 * n:128 * (n + 1), sl],
                              xoutb[i % 2][:])
                        w.mark(("ost", i))

            wk = Walker(ctx)
            wk.run(walkO)



BLOCKS = {}

BLOCKS["L3_ffn"] = (
    ["L3_ffn_w1", "L3_ffn_b1", "L3_ffn_w2", "L3_ffn_b2"],
    lambda ctx, xi, xo, w: emit_l3_ffn(ctx, xi, xo, {
        "w1": w["L3_ffn_w1"], "b1": w["L3_ffn_b1"],
        "w2": w["L3_ffn_w2"], "b2": w["L3_ffn_b2"]}),
)

BLOCKS["L0_attn"] = (
    ["L0_attn_Wq", "L0_attn_bq", "L0_attn_Wk", "L0_attn_bk", "L0_attn_Wv",
     "L0_attn_bv", "L0_attn_Wo", "L0_attn_bo", "L0_attn_scale"],
    lambda ctx, xi, xo, w: emit_softmax_attn(ctx, xi, xo, {
        "Wq": w["L0_attn_Wq"], "bq": w["L0_attn_bq"],
        "Wk": w["L0_attn_Wk"], "bk": w["L0_attn_bk"],
        "Wv": w["L0_attn_Wv"], "bv": w["L0_attn_bv"],
        "Wo": w["L0_attn_Wo"], "bo": w["L0_attn_bo"],
        "scale": w["L0_attn_scale"]},
        causal=True, out_proj=True, fixed_scale=None),
)

BLOCKS["L2_attn"] = (
    ["L2_attn_wq_w", "L2_attn_wq_b", "L2_attn_wk_w", "L2_attn_wk_b",
     "L2_attn_wv_w", "L2_attn_wv_b"],
    lambda ctx, xi, xo, w: emit_softmax_attn(ctx, xi, xo, {
        "Wq": w["L2_attn_wq_w"], "bq": w["L2_attn_wq_b"],
        "Wk": w["L2_attn_wk_w"], "bk": w["L2_attn_wk_b"],
        "Wv": w["L2_attn_wv_w"], "bv": w["L2_attn_wv_b"]},
        causal=False, out_proj=False, fixed_scale=0.125),
)

BLOCKS["L0_ffn"] = (
    ["L0_ffn_router_w", "L0_ffn_w1", "L0_ffn_w2", "L0_ffn_core_prefix"],
    lambda ctx, xi, xo, w: emit_l0_ffn(ctx, xi, xo, {
        "router_w": w["L0_ffn_router_w"], "w1": w["L0_ffn_w1"],
        "w2": w["L0_ffn_w2"], "core_prefix": w["L0_ffn_core_prefix"]}),
)

BLOCKS["L3_attn"] = (
    ["L3_attn_Wqkv", "L3_attn_bqkv", "L3_attn_Wout", "L3_attn_bout",
     "L3_attn_omega", "L3_attn_quad_nodes", "L3_attn_quad_weights"],
    lambda ctx, xi, xo, w: emit_l3_attn(ctx, xi, xo, {
        "Wqkv": w["L3_attn_Wqkv"], "bqkv": w["L3_attn_bqkv"],
        "Wout": w["L3_attn_Wout"], "bout": w["L3_attn_bout"],
        "omega": w["L3_attn_omega"], "quad_nodes": w["L3_attn_quad_nodes"],
        "quad_weights": w["L3_attn_quad_weights"]}),
)

BLOCKS["L1_ffn"] = (
    ["L1_ffn_gate_w", "L1_ffn_gate_b", "L1_ffn_expert_w", "L1_ffn_expert_b"],
    lambda ctx, xi, xo, w: emit_l1_ffn(ctx, xi, xo, {
        "gate_w": w["L1_ffn_gate_w"], "gate_b": w["L1_ffn_gate_b"],
        "expert_w": w["L1_ffn_expert_w"], "expert_b": w["L1_ffn_expert_b"]}),
)

BLOCKS["L1_attn"] = (
    ["L1_attn_W_hg", "L1_attn_W_out"],
    lambda ctx, xi, xo, w: emit_l1_attn(ctx, xi, xo, {
        "W_hg": w["L1_attn_W_hg"], "W_out": w["L1_attn_W_out"]}),
)

BLOCKS["L2_ffn"] = (
    ["L2_ffn_cn_w1", "L2_ffn_cn_b1", "L2_ffn_cn_w2", "L2_ffn_cn_b2",
     "L2_ffn_cn_w3", "L2_ffn_cn_b3", "L2_ffn_cn_w4", "L2_ffn_cn_b4",
     "L2_ffn_gc_w1", "L2_ffn_gc_b1", "L2_ffn_gc_w2", "L2_ffn_gc_b2",
     "L2_ffn_gc_w3", "L2_ffn_gc_b3", "L2_ffn_adaptive_range"],
    lambda ctx, xi, xo, w: emit_l2_ffn(ctx, xi, xo, {
        k: w["L2_ffn_" + k] for k in
        ["cn_w1", "cn_b1", "cn_w2", "cn_b2", "cn_w3", "cn_b3", "cn_w4",
         "cn_b4", "gc_w1", "gc_b1", "gc_w2", "gc_b2", "gc_w3", "gc_b3",
         "adaptive_range"]}),
)

INPUT_SHAPES = {
    "L3_ffn_w1": (4096, 1024), "L3_ffn_b1": (4096,),
    "L3_ffn_w2": (1024, 4096), "L3_ffn_b2": (1024,),
    "L2_ffn_cn_w1": (1024, 1024), "L2_ffn_cn_b1": (1024,),
    "L2_ffn_cn_w2": (1024, 512), "L2_ffn_cn_b2": (512,),
    "L2_ffn_cn_w3": (512, 256), "L2_ffn_cn_b3": (256,),
    "L2_ffn_cn_w4": (256, 1), "L2_ffn_cn_b4": (1,),
    "L2_ffn_gc_w1": (1024, 512), "L2_ffn_gc_b1": (512,),
    "L2_ffn_gc_w2": (512, 256), "L2_ffn_gc_b2": (256,),
    "L2_ffn_gc_w3": (256, 1), "L2_ffn_gc_b3": (1,),
    "L2_ffn_adaptive_range": (1,),
    "L1_ffn_gate_w": (8, 1024), "L1_ffn_gate_b": (8,),
    "L1_ffn_expert_w": (8, 1024, 1024), "L1_ffn_expert_b": (8, 1024),
    "L1_attn_W_hg": (1024, 4096), "L1_attn_W_out": (2048, 1024),
    "L0_attn_Wq": (1024, 1024), "L0_attn_bq": (1024,),
    "L0_attn_Wk": (1024, 1024), "L0_attn_bk": (1024,),
    "L0_attn_Wv": (1024, 1024), "L0_attn_bv": (1024,),
    "L0_attn_Wo": (1024, 1024), "L0_attn_bo": (1024,),
    "L0_attn_scale": (1,),
    "L2_attn_wq_w": (1024, 1024), "L2_attn_wq_b": (1024,),
    "L2_attn_wk_w": (1024, 1024), "L2_attn_wk_b": (1024,),
    "L2_attn_wv_w": (1024, 1024), "L2_attn_wv_b": (1024,),
    "L0_ffn_router_w": (1024, 8), "L0_ffn_w1": (8, 1024, 4096),
    "L0_ffn_w2": (8, 4096, 1024), "L0_ffn_core_prefix": (8,),
    "L3_attn_Wqkv": (1024, 3072), "L3_attn_bqkv": (3072,),
    "L3_attn_Wout": (1024, 1024), "L3_attn_bout": (1024,),
    "L3_attn_omega": (2, 16, 64, 32), "L3_attn_quad_nodes": (2,),
    "L3_attn_quad_weights": (2,),
}


def build_program(block_names):
    nc = bass.Bass("TRN2", target_bir_lowering=False,
                   detect_race_conditions=False)
    x_in = nc.dram_tensor("x_in", [D, S], F32, kind="ExternalInput").ap()
    x_out = nc.dram_tensor("x_out", [D, S], F32, kind="ExternalOutput").ap()
    wap = {}
    for bname in block_names:
        for wname in BLOCKS[bname][0]:
            wap[wname] = nc.dram_tensor(
                wname, list(INPUT_SHAPES[wname]), F32, kind="ExternalInput").ap()
    stack = ExitStack()
    with stack:
        block = stack.enter_context(nc.Block())
        ctx = Ctx(nc, stack, block)
        n = len(block_names)
        xs = [x_in]
        for i in range(n - 1):
            xs.append(ctx.dram([D, S], F32, "xmid"))
        xs.append(x_out)
        for i, bname in enumerate(block_names):
            BLOCKS[bname][1](ctx, xs[i], xs[i + 1], wap)

        for tname in DEBUG_TAPS:
            if tname in ctx.taps:
                src_ap = ctx.taps[tname]
                th = src_ap.tensor
                dst = nc.dram_tensor("tap_" + tname, list(th.shape), th.dtype,
                                     kind="ExternalOutput").ap()
                gq = ctx.sem("q_gp")
                rows, cols = th.shape
                bb = ctx.sbuf([min(128, rows), cols], th.dtype, "tapb")

                def tap_f(gpsimd, src_ap=src_ap, dst=dst, bb=bb, rows=rows):
                    for sname, s in ctx._sems.items():
                        if sname != "q_gp":
                            s.wait(gpsimd)
                    for r0 in range(0, rows, 128):
                        r1 = min(r0 + 128, rows)
                        gq.inc(gpsimd.dma_start(out=bb[0:r1 - r0, :],
                                                in_=src_ap[r0:r1, :]), 16)
                        gpsimd.wait_ge(gq.h, gq.n)
                        gq.inc(gpsimd.dma_start(out=dst[r0:r1, :],
                                                in_=bb[0:r1 - r0, :]), 16)
                        gpsimd.wait_ge(gq.h, gq.n)
                block.gpsimd(tap_f)

        def fin(sync):
            for s in ctx._sems.values():
                s.wait(sync)
        block.sync(fin)
    return nc


BLOCK_ORDER = ["L0_attn", "L0_ffn", "L1_attn", "L1_ffn",
               "L2_attn", "L2_ffn", "L3_attn", "L3_ffn"]


def _in_maps_for(bnames, cur):
    maps = []
    for b in range(8):
        m = {"x_in": cur[b]}
        for bname in bnames:
            for wname in BLOCKS[bname][0]:
                if wname.endswith("core_prefix"):
                    m[wname] = (np.arange(8) < b).astype(np.float32)
                else:
                    m[wname] = _W[wname]
        maps.append(m)
    return maps


def kernel(**inputs):
    global _W
    x = inputs["L0_attn_x"]
    _W = {k: np.ascontiguousarray(np.asarray(v, dtype=np.float32))
          for k, v in inputs.items() if k != "L0_attn_x"}
    cur = [np.ascontiguousarray(x[b].T) for b in range(8)]
    nc = build_program(BLOCK_ORDER)
    res = run_bass_kernel_spmd(nc, _in_maps_for(BLOCK_ORDER, cur),
                               core_ids=list(range(8)))
    cur = [np.ascontiguousarray(np.asarray(res.results[b]["x_out"]))
           for b in range(8)]
    return np.stack([c.T for c in cur], 0).astype(np.float32)
